# version 03
# baseline (speedup 1.0000x reference)
import contextlib
import math
import numpy as np
import concourse.bass as bass
import concourse.mybir as mybir
from concourse.bass_utils import run_bass_kernel_spmd
from concourse.alu_op_type import AluOpType as Op

F32 = mybir.dt.float32
BF16 = mybir.dt.bfloat16
AF = mybir.ActivationFunctionType
AX = mybir.AxisListType

B, S, D = 4, 4096, 1024
NQ = 2048
H = 16
DQ = DKV = 512
FF = 4096
EPS = 1e-5
SCALE = float(np.sqrt(np.float32(H + DQ + DKV)))
INV_SCALE = 1.0 / SCALE
PI = math.pi
TWO_PI = 2 * math.pi
HALF_PI = 0.5 * math.pi
HALF_PI_F32 = float(np.float32(0.5 * math.pi))
HALF_PI_LO = HALF_PI - HALF_PI_F32
LN1E4 = 9.210340371976184

ENGS = ("sync", "tensor", "vector", "scalar", "gpsimd")


class Prog:

    def __init__(self):
        self.steps = []
        self.barrier_toks = []

    def do(self, eng, fn, deps=(), dma=None):
        t = len(self.steps)
        dd = [d for d in deps if d is not None]
        dd.extend(self.barrier_toks)
        self.steps.append((eng, fn, tuple(dd), dma))
        return t

    def barrier(self):
        last = {}
        for i, (eng, _f, _d, dma) in enumerate(self.steps):
            last[eng if dma is None else ("ch", dma)] = i
        self.barrier_toks = list(last.values())

    def emit(self, nc, gstack, block, csem):
        steps = self.steps
        need = [False] * len(steps)
        for (_e, _f, deps, _dma) in steps:
            for d in deps:
                need[d] = True
        chans = []
        for (_e, _f, _d, dma) in steps:
            if dma is not None and dma not in chans:
                chans.append(dma)
        chsem = {c: gstack.enter_context(nc.semaphore("ch_" + c)) for c in chans}
        meta = [None] * len(steps)
        ccount = {e: 0 for e in ENGS}
        chcount = {c: 0 for c in chans}
        eng_chfinal = {e: {} for e in ENGS}
        for i, (eng, _f, _deps, dma) in enumerate(steps):
            if dma is not None:
                chcount[dma] += 16
                meta[i] = ("ch", dma, chcount[dma])
                eng_chfinal[eng][dma] = chcount[dma]
            else:
                if need[i]:
                    ccount[eng] += 1
                meta[i] = ("c", eng, ccount[eng])

        def body_for(E):
            mine = [(i, steps[i]) for i in range(len(steps)) if steps[i][0] == E]

            def body(engobj):
                ensured = {}
                for i, (_e, fn, deps, dma) in mine:
                    for d in deps:
                        kind, key2, dval = meta[d]
                        key = (kind, key2)
                        if ensured.get(key, 0) >= dval:
                            continue
                        sem = csem[key2] if kind == "c" else chsem[key2]
                        engobj.wait_ge(sem, dval)
                        ensured[key] = dval
                    inst = fn(engobj)
                    if dma is not None:
                        inst.then_inc(chsem[dma], 16)
                    elif need[i]:
                        inst.then_inc(csem[E], 1)
                for c, v in eng_chfinal[E].items():
                    engobj.wait_ge(chsem[c], v)

            return body

        for E in ENGS:
            getattr(block, E)(body_for(E))


class Banks:
    def __init__(self):
        self.readers = {b: [] for b in range(8)}
        self.writer = {b: None for b in range(8)}

    def pre_write(self, b):
        r = list(self.readers[b])
        if self.writer[b] is not None:
            r.append(self.writer[b])
        return r

    def wrote(self, b, tok):
        self.writer[b] = tok
        self.readers[b] = []

    def read(self, b, tok):
        self.readers[b].append(tok)


def build_program(dump=()):
    nc = bass.Bass("TRN2", target_bir_lowering=False)
    p = Prog()
    bk = Banks()
    LU = {}

    xv = nc.dram_tensor("xv", [S, D], F32, kind="ExternalInput").ap()
    seg = nc.dram_tensor("seg", [2], F32, kind="ExternalInput").ap()
    win = {}
    for nm, shp in [("wdq", [D, DQ]), ("wuq", [DQ, D]), ("wqr", [DQ, D]),
                    ("wkr", [D, D]), ("wdkv", [D, DKV]), ("wuk", [DKV, D]),
                    ("wuv", [DKV, D]), ("wo", [D, D]), ("wr", [8, D]),
                    ("w1", [D, FF]), ("w2", [FF, D]),
                    ("bdq", [DQ]), ("buq", [D]), ("bqr", [D]), ("bkr", [D]),
                    ("bdkv", [DKV]), ("buk", [D]), ("buv", [D]), ("bo", [D]),
                    ("b1", [FF]), ("b2", [D])]:
        win[nm] = nc.dram_tensor(nm, shp, F32, kind="ExternalInput").ap()
    out_hbm = nc.dram_tensor("out", [NQ, D], F32, kind="ExternalOutput").ap()

    dumps = {}

    def dump_t(nm, shp):
        if nm in dump:
            dumps[nm] = nc.dram_tensor("dump_" + nm, shp, F32, kind="ExternalOutput").ap()
        return dumps.get(nm)

    xnT_hbm = nc.dram_tensor("xnT_hbm", [D, S], F32, kind="Internal").ap()
    kT_hbm = nc.dram_tensor("kT_hbm", [D, S], F32, kind="Internal").ap()
    v_hbm = nc.dram_tensor("v_hbm", [S, H * 65], F32, kind="Internal").ap()
    qT_hbm = nc.dram_tensor("qT_hbm", [D, NQ], F32, kind="Internal").ap()
    res_hbm = nc.dram_tensor("res_hbm", [NQ, D], F32, kind="Internal").ap()

    d_xnT = dump_t("xnT", [D, S])
    d_kT = dump_t("kT", [D, S])
    d_v = dump_t("v", [S, H * 65])
    d_qT = dump_t("qT", [D, NQ])
    d_OT = dump_t("OT", [D, NQ])
    d_res = dump_t("res", [NQ, D])
    d_negm = dump_t("negm", [1, 1])

    gstack = contextlib.ExitStack()

    def mk_sb(stk):
        return lambda nm, shp, dt=F32: stk.enter_context(nc.sbuf_tensor(nm, shp, dt)).ap()

    sb = mk_sb(gstack)

    ident = sb("ident", [128, 128])
    ones8 = sb("ones8", [8, 128])
    theta_bc = sb("theta_bc", [128, 512])
    halfpi_col = sb("halfpi_col", [128, 1])
    segcol = sb("segcol", [128, 2])
    seg_sb = sb("seg_sb", [1, 2])
    negM = sb("negM", [128, 1])
    e_sb = sb("e_sb", [128, 8 * 16])
    kmax_c = sb("kmax_c", [16, 1])
    qmax_c = sb("qmax_c", [16, 1])
    t16 = sb("t16", [16, 1])
    p7_sb = sb("p7_sb", [128, 16])
    bcol = {}
    for nm, w in [("bdq", 4), ("bdkv", 4), ("buq", 8), ("buk", 8), ("b1", 32)]:
        bcol[nm] = sb(nm + "_col", [128, w])
    buqs = sb("buqs_col", [128, 8])
    ln_c = [sb(f"ln_c{i}", [128, 1]) for i in range(4)]
    ln_m = sb("ln_m", [128, 1])
    sq_scr = sb("sq_scr", [128, 1024])
    lnout = sb("lnout", [128, 1024])
    mrow = sb("mrow", [1, 64])
    xt = [sb(f"xt{i}", [128, 1024]) for i in range(2)]
    xn = sb("xn", [128, 1024])
    res_t = sb("res_t", [128, 1024])
    ao_t = sb("ao_t", [128, 1024])

    psum = gstack.enter_context(nc.psum_tensor("pall", [128, 4096], F32)).ap()
    PB = lambda b: psum[:, 512 * b:512 * (b + 1)]

    csem = {e: gstack.enter_context(nc.semaphore("c_" + e)) for e in ENGS}

    t_id0 = p.do("gpsimd", lambda e: e.memset(ident[:], 0.0))
    t_id = p.do("gpsimd", lambda e: e.affine_select(
        out=ident[:], in_=ident[:], compare_op=Op.not_equal, fill=1.0,
        base=0, pattern=[[-1, 128]], channel_multiplier=1), deps=[t_id0])
    t_ones = p.do("gpsimd", lambda e: e.memset(ones8[:], 1.0))
    t_hp0 = p.do("gpsimd", lambda e: e.memset(halfpi_col[:], HALF_PI_F32))
    t_hp = p.do("vector", lambda e: e.tensor_scalar_add(halfpi_col[:], halfpi_col[:],
                                                        HALF_PI_LO), deps=[t_hp0])
    t_kin = p.do("gpsimd", lambda e: e.memset(kmax_c[:], -3.0e38))
    t_qin = p.do("gpsimd", lambda e: e.memset(qmax_c[:], -3.0e38))
    t_e0 = p.do("gpsimd", lambda e: e.memset(e_sb[:], 0.0))
    t_es = []
    for ic in range(8):
        for sub in range(2):
            h = 2 * ic + sub
            t_es.append(p.do("gpsimd", lambda e, ic=ic, sub=sub, h=h: e.memset(
                e_sb[64 * sub:64 * sub + 64, 16 * ic + h:16 * ic + h + 1], 1.0),
                deps=[t_e0]))
    t_th0 = p.do("gpsimd", lambda e: e.iota(
        theta_bc[0:1, :], pattern=[[1, 512]], base=0, channel_multiplier=0,
        allow_small_or_imprecise_dtypes=True))
    t_th1 = p.do("scalar", lambda e: e.activation(
        theta_bc[0:1, :], theta_bc[0:1, :], AF.Exp, bias=0.0, scale=-2.0 * LN1E4),
        deps=[t_th0])
    t_th2 = p.do("vector", lambda e: e.tensor_scalar_mul(
        theta_bc[0:1, :], theta_bc[0:1, :], 1.0 / 512.0), deps=[t_th1])
    t_th3 = p.do("tensor", lambda e: e.matmul(
        PB(0)[:, 0:512], ones8[0:1, :], theta_bc[0:1, :], start=True, stop=True),
        deps=[t_ones, t_th2] + bk.pre_write(0))
    bk.wrote(0, t_th3)
    t_th = p.do("vector", lambda e: e.tensor_copy(theta_bc[:], PB(0)[:, 0:512]),
                deps=[t_th3])
    bk.read(0, t_th)

    t_b = {}
    for nm in ["bdq", "bdkv", "buq", "buk", "b1"]:
        t_b[nm] = p.do("sync", lambda e, nm=nm: e.dma_start(
            out=bcol[nm][:], in_=win[nm].rearrange("(c p) -> p c", p=128),
            allow_slow_non_contiguous=True), dma="bias0")
    t_sg0 = p.do("sync", lambda e: e.dma_start(out=seg_sb[:], in_=seg.unsqueeze(0)),
                 dma="bias0")
    for nm in ["bdq", "bdkv", "buq", "buk", "b1"]:
        t_b[nm] = t_sg0
    t_sg1 = p.do("tensor", lambda e: e.matmul(
        PB(1)[:, 0:2], ones8[0:1, :], seg_sb[:], start=True, stop=True),
        deps=[t_sg0, t_ones] + bk.pre_write(1))
    bk.wrote(1, t_sg1)
    t_seg = p.do("vector", lambda e: e.tensor_copy(segcol[:], PB(1)[:, 0:2]), deps=[t_sg1])
    bk.read(1, t_seg)

    t_buqs = p.do("vector", lambda e: e.tensor_scalar_mul(buqs[:], bcol["buq"][:], INV_SCALE),
                  deps=[t_b["buq"]])

    brow = {}

    def load_brow(stk, nm, ch):
        t = mk_sb(stk)(nm + "_row", [1, D])
        brow[nm] = t
        t_b[nm] = p.do("sync", lambda e, nm=nm, t=t: e.dma_start(
            out=t[:], in_=win[nm].unsqueeze(0)), dma=ch)

    wsb = {}

    def load_w(stk, nm, I, O):
        t = mk_sb(stk)(nm + "_sb", [128, (I // 128) * O])
        toks = []
        for ic in range(I // 128):
            toks.append(p.do("gpsimd", lambda e, ic=ic, t=t, nm=nm, O=O: e.dma_start(
                out=t[:, O * ic:O * (ic + 1)], in_=win[nm][128 * ic:128 * (ic + 1), :]),
                dma="w_" + nm))
        wsb[nm] = (t, O)
        return toks

    WC = lambda nm, ic, lo, hi: wsb[nm][0][:, wsb[nm][1] * ic + lo: wsb[nm][1] * ic + hi]

    def mm_group(bank, mms, extra_deps=()):
        pre = bk.pre_write(bank) + list(extra_deps)
        tok = None
        nmm = len(mms)
        for j, (lf, rf, dd, istr, osl) in enumerate(mms):
            o = PB(bank)[:, 0:512] if osl is None else osl
            tok = p.do("tensor", lambda e, lf=lf, rf=rf, o=o, j=j, istr=istr, nmm=nmm: e.matmul(
                o, lf(), rf(), start=(j == 0), stop=(j == nmm - 1),
                is_transpose=True if istr else None),
                deps=(pre if j == 0 else []) + list(dd))
        bk.wrote(bank, tok)
        return tok

    def ln_stats(src_tok, src_ap, dep_extra=()):
        t1 = p.do("vector", lambda e: e.reduce_sum(ln_c[0][:], src_ap, AX.X),
                  deps=[src_tok] + list(dep_extra))
        t2 = p.do("scalar", lambda e: e.activation(
            sq_scr[:], src_ap, AF.Square, bias=0.0, scale=1.0, accum_out=ln_c[2][:]),
            deps=[src_tok, LU.get("sq_scr")] + list(dep_extra))
        LU["sq_scr"] = t2
        t3 = p.do("vector", lambda e: e.tensor_scalar_mul(ln_c[1][:], ln_c[0][:], 1.0 / 1024.0),
                  deps=[t1])
        t4 = p.do("vector", lambda e: e.tensor_scalar_mul(ln_c[2][:], ln_c[2][:], 1.0 / 1024.0),
                  deps=[t2])
        t5 = p.do("vector", lambda e: e.tensor_tensor(ln_m[:], ln_c[1][:], ln_c[1][:], Op.mult),
                  deps=[t3])
        t6 = p.do("vector", lambda e: e.tensor_tensor(ln_c[2][:], ln_c[2][:], ln_m[:],
                                                      Op.subtract), deps=[t4, t5])
        t7 = p.do("vector", lambda e: e.tensor_scalar_add(ln_c[2][:], ln_c[2][:], EPS), deps=[t6])
        t8 = p.do("scalar", lambda e: e.activation(ln_c[3][:], ln_c[2][:], AF.Sqrt,
                                                   bias=0.0, scale=1.0), deps=[t7])
        t9 = p.do("vector", lambda e: e.reciprocal(ln_c[3][:], ln_c[3][:]), deps=[t8])
        t10 = p.do("vector", lambda e: e.tensor_tensor(ln_m[:], ln_c[1][:], ln_c[3][:], Op.mult),
                   deps=[t9])
        t11 = p.do("vector", lambda e: e.tensor_scalar_mul(ln_m[:], ln_m[:], -1.0), deps=[t10])
        return t11

    astack = contextlib.ExitStack()
    asb = mk_sb(astack)
    xnT_t = asb("xnT_t", [128, 8 * 512])
    ckvT_t = asb("ckvT_t", [128, 4 * 512])
    outT_t = asb("outT_t", [128, 8 * 512])
    big_t = asb("big_t", [128, 4096])
    v_t = asb("v_t", [128, 4 * 1040])
    cos_t = asb("cos_t", [128, 512])
    sin_t = asb("sin_t", [128, 512])
    pos_c = asb("pos_c", [128, 1])
    rr_t = asb("rr_t", [128, 512])
    k1_t = asb("k1_t", [128, 512])
    k2_t = asb("k2_t", [128, 512])
    tmpa = asb("tmpa", [128, 512])
    tmpb = asb("tmpb", [128, 512])

    def cos_sin(r0):
        base = r0 % 2048
        si = r0 // 2048
        ta = p.do("gpsimd", lambda e, base=base: e.iota(
            pos_c[:], pattern=[[0, 1]], base=base, channel_multiplier=1,
            allow_small_or_imprecise_dtypes=True), deps=[LU.get("pos")])
        tb = p.do("vector", lambda e, si=si: e.tensor_tensor(
            pos_c[:], pos_c[:], segcol[:, si:si + 1], Op.add), deps=[ta, t_seg])
        tr = p.do("vector", lambda e: e.tensor_scalar(
            rr_t[:], theta_bc[:], pos_c[:], None, Op.mult), deps=[tb, t_th, LU.get("rr")])
        tk1 = p.do("vector", lambda e: e.tensor_scalar(k1_t[:], rr_t[:], PI, None, Op.is_ge),
                   deps=[tr, LU.get("k1")])
        tk1b = p.do("vector", lambda e: e.tensor_scalar_mul(k1_t[:], k1_t[:], TWO_PI), deps=[tk1])
        tk1c = p.do("vector", lambda e: e.tensor_tensor(tmpa[:], rr_t[:], k1_t[:], Op.subtract),
                    deps=[tk1b, LU.get("tmpa")])
        tsin = p.do("scalar", lambda e: e.activation(sin_t[:], tmpa[:], AF.Sin,
                                                     bias=0.0, scale=1.0),
                    deps=[tk1c, LU.get("sin_w")])
        tq1 = p.do("vector", lambda e: e.tensor_scalar(k2_t[:], rr_t[:], HALF_PI, None,
                                                       Op.is_ge), deps=[tr, LU.get("k2")])
        tq2 = p.do("vector", lambda e: e.tensor_scalar(tmpb[:], rr_t[:], 5.0 * HALF_PI, None,
                                                       Op.is_ge), deps=[tr, LU.get("tmpb")])
        tq3 = p.do("vector", lambda e: e.tensor_tensor(k2_t[:], k2_t[:], tmpb[:], Op.add),
                   deps=[tq1, tq2])
        tq4 = p.do("vector", lambda e: e.tensor_scalar_mul(k2_t[:], k2_t[:], TWO_PI), deps=[tq3])
        tq5 = p.do("vector", lambda e: e.tensor_tensor(tmpb[:], rr_t[:], k2_t[:], Op.subtract),
                   deps=[tq4])
        tcos = p.do("scalar", lambda e: e.activation(cos_t[:], tmpb[:], AF.Sin,
                                                     bias=halfpi_col[:], scale=1.0),
                    deps=[tq5, t_hp, LU.get("cos_w")])
        LU["rr"] = tq5
        LU["k1"] = tk1c
        LU["k2"] = tq5
        LU["tmpa"] = tsin
        LU["tmpb"] = tcos
        LU["pos"] = tr
        return tcos, tsin

    def rope_apply(ps_ap, oh, dst_ap, dep_group, tcos, tsin):
        co = cos_t[:, 256 * oh:256 * oh + 256]
        si = sin_t[:, 256 * oh:256 * oh + 256]
        ev, od = ps_ap[:, 0:512:2], ps_ap[:, 1:512:2]
        dev, dod = dst_ap[:, 0:512:2], dst_ap[:, 1:512:2]
        t1 = p.do("vector", lambda e: e.tensor_tensor(tmpa[:, 0:256], ev, co, Op.mult),
                  deps=[dep_group, tcos, LU.get("tmpa")])
        t2 = p.do("vector", lambda e: e.tensor_tensor(tmpb[:, 0:256], od, si, Op.mult),
                  deps=[dep_group, tsin, LU.get("tmpb")])
        t3 = p.do("vector", lambda e: e.tensor_tensor(dev, tmpa[:, 0:256], tmpb[:, 0:256],
                                                      Op.subtract), deps=[t1, t2])
        t4 = p.do("vector", lambda e: e.tensor_tensor(tmpa[:, 0:256], od, co, Op.mult),
                  deps=[dep_group, t3])
        t5 = p.do("vector", lambda e: e.tensor_tensor(tmpb[:, 0:256], ev, si, Op.mult),
                  deps=[dep_group, t3])
        t6 = p.do("vector", lambda e: e.tensor_tensor(dod, tmpa[:, 0:256], tmpb[:, 0:256],
                                                      Op.add), deps=[t4, t5])
        LU["tmpa"] = t6
        LU["tmpb"] = t6
        return t6, [t1, t2, t4, t5]

    def ln_transpose_tile(r0_tile, spill):
        toks_all = []
        for c in range(4):
            buf = xt[c % 2]
            r0 = r0_tile + 128 * c
            tl = p.do("sync", lambda e, buf=buf, r0=r0: e.dma_start(
                out=buf[:], in_=xv[r0:r0 + 128, :]), deps=[LU.get(("xt", c % 2))],
                dma=f"xt{c % 2}")
            ts = ln_stats(tl, buf[:])
            txn = p.do("scalar", lambda e, buf=buf: e.activation(
                xn[:], buf[:], AF.Identity, bias=ln_m[:], scale=ln_c[3][:]),
                deps=[ts, LU.get("xn")])
            LU[("xt", c % 2)] = txn
            ev_last = None
            for half in range(2):
                bkid = half
                mms = []
                for jj in range(4):
                    j = 4 * half + jj
                    mms.append(((lambda j=j: xn[:, 128 * j:128 * j + 128]),
                                (lambda: ident[:]),
                                [txn, t_id], True, PB(bkid)[:, 128 * jj:128 * jj + 128]))
                g = mm_group(bkid, mms)
                for jj in range(4):
                    j = 4 * half + jj
                    tv = p.do("vector", lambda e, bkid=bkid, jj=jj, j=j, c=c: e.tensor_copy(
                        xnT_t[:, 512 * j + 128 * c: 512 * j + 128 * c + 128],
                        PB(bkid)[:, 128 * jj:128 * jj + 128]),
                        deps=[g, LU.get("xnT_t"), LU.get("xnT_dump"), LU.get("xnT_sp_last")])
                    bk.read(bkid, tv)
                    ev_last = tv
                    toks_all.append(tv)
            LU["xn"] = ev_last
        if spill:
            for j in range(8):
                LU["xnT_sp_last"] = p.do("sync", lambda e, j=j, r0_tile=r0_tile: e.dma_start(
                    out=xnT_hbm[128 * j:128 * j + 128, r0_tile:r0_tile + 512],
                    in_=xnT_t[:, 512 * j:512 * j + 512]), deps=list(toks_all),
                    dma="xnT_sp")
                if d_xnT is not None:
                    LU["xnT_dump"] = p.do("sync", lambda e, j=j, r0_tile=r0_tile: e.dma_start(
                        out=d_xnT[128 * j:128 * j + 128, r0_tile:r0_tile + 512],
                        in_=xnT_t[:, 512 * j:512 * j + 512]), deps=list(toks_all),
                        dma="dpx")
        return toks_all

    wa1 = contextlib.ExitStack()
    load_brow(wa1, "bkr", "biasA1")
    load_brow(wa1, "buv", "biasA1")
    t_b["bkr"] = t_b["buv"]
    tw_dkv = load_w(wa1, "wdkv", D, DKV)
    tw_uk = load_w(wa1, "wuk", DKV, D)
    tw_uv = load_w(wa1, "wuv", DKV, D)
    tw_kr = load_w(wa1, "wkr", D, D)

    vones = []
    for c in range(4):
        for hg in range(16):
            vones.append(p.do("gpsimd", lambda e, c=c, hg=hg: e.memset(
                v_t[:, 1040 * c + 65 * hg + 64: 1040 * c + 65 * hg + 65], 1.0)))
    t_vones = vones[-1]

    tkm = None
    for st in range(8):
        r0_tile = 512 * st
        xnT_toks = ln_transpose_tile(r0_tile, spill=True)
        ckv_evac = []
        for oc in range(4):
            bkid = 2 + (oc % 2)
            mms = []
            for ic in range(8):
                mms.append(((lambda ic=ic, oc=oc: WC("wdkv", ic, 128 * oc, 128 * oc + 128)),
                            (lambda ic=ic: xnT_t[:, 512 * ic:512 * ic + 512]),
                            [tw_dkv[-1], xnT_toks[-1]], False, None))
            g = mm_group(bkid, mms)
            tv = p.do("scalar", lambda e, oc=oc, bkid=bkid: e.activation(
                ckvT_t[:, 512 * oc:512 * oc + 512], PB(bkid)[:, 0:512], AF.Identity,
                bias=bcol["bdkv"][:, oc:oc + 1], scale=1.0),
                deps=[g, t_b["bdkv"], LU.get("ckvT")])
            bk.read(bkid, tv)
            ckv_evac.append(tv)
        LU["ckvT"] = ckv_evac[-1]

        rope_done = []
        for c in range(4):
            tcos, tsin = cos_sin(r0_tile + 128 * c)
            for oh in range(2):
                bkid = 4 + oh
                mms = []
                for ic in range(8):
                    mms.append(((lambda ic=ic, c=c: xnT_t[:, 512 * ic + 128 * c:
                                                          512 * ic + 128 * c + 128]),
                                (lambda ic=ic, oh=oh: WC("wkr", ic, 512 * oh, 512 * oh + 512)),
                                [tw_kr[-1], xnT_toks[-1]], False, None))
                mms.append(((lambda: ones8[0:1, :]),
                            (lambda oh=oh: brow["bkr"][:, 512 * oh:512 * oh + 512]),
                            [t_ones, t_b["bkr"]], False, None))
                g = mm_group(bkid, mms, extra_deps=[LU.get(("krp", c, oh))])
                LU["xnT_t"] = g
                dst = big_t[:, 1024 * c + 512 * oh: 1024 * c + 512 * oh + 512]
                t6, readers = rope_apply(PB(bkid)[:, 0:512], oh, dst, g, tcos, tsin)
                for rt in readers + [t6]:
                    bk.read(bkid, rt)
                LU[("krp", c, oh)] = t6
                rope_done.append(t6)

        kT_evac = []
        for oc in range(8):
            bkid = 6 + (oc % 2)
            mms = []
            for cc in range(4):
                mms.append(((lambda cc=cc, oc=oc: WC("wuk", cc, 128 * oc, 128 * oc + 128)),
                            (lambda cc=cc: ckvT_t[:, 512 * cc:512 * cc + 512]),
                            [tw_uk[-1], ckv_evac[-1]], False, None))
            for c in range(4):
                mms.append(((lambda c=c, oc=oc: big_t[:, 1024 * c + 128 * oc:
                                                      1024 * c + 128 * oc + 128]),
                            (lambda: ident[:]),
                            [rope_done[-1], t_id], True, PB(bkid)[:, 128 * c:128 * c + 128]))
            g = mm_group(bkid, mms)
            tv = p.do("scalar", lambda e, oc=oc, bkid=bkid: e.activation(
                outT_t[:, 512 * oc:512 * oc + 512], PB(bkid)[:, 0:512], AF.Identity,
                bias=bcol["buk"][:, oc:oc + 1], scale=1.0),
                deps=[g, t_b["buk"], LU.get(("outT", oc)), LU.get(("outT_dp", oc))])
            bk.read(bkid, tv)
            tsp = p.do("sync", lambda e, oc=oc, r0_tile=r0_tile: e.dma_start(
                out=kT_hbm[128 * oc:128 * oc + 128, r0_tile:r0_tile + 512],
                in_=outT_t[:, 512 * oc:512 * oc + 512]), deps=[tv], dma=f"osp{oc}")
            LU[("outT", oc)] = tsp
            if d_kT is not None:
                LU[("outT_dp", oc)] = p.do("sync", lambda e, oc=oc, r0_tile=r0_tile: e.dma_start(
                    out=d_kT[128 * oc:128 * oc + 128, r0_tile:r0_tile + 512],
                    in_=outT_t[:, 512 * oc:512 * oc + 512]), deps=[tv], dma=f"dpk{oc}")
            kT_evac.append(tv)

        tsq = p.do("scalar", lambda e: e.activation(big_t[:], outT_t[:], AF.Square,
                                                    bias=0.0, scale=1.0),
                   deps=kT_evac + [rope_done[-1], LU.get("big")])
        mms = []
        for ic in range(8):
            mms.append(((lambda ic=ic: e_sb[:, 16 * ic:16 * ic + 16]),
                        (lambda ic=ic: big_t[:, 512 * ic:512 * ic + 512]),
                        [tsq] + ([t_es[-1]] if ic == 0 else []), False, PB(2)[0:16, 0:512]))
        g = mm_group(2, mms)
        trm = p.do("vector", lambda e: e.reduce_max(t16[:], PB(2)[0:16, 0:512], AX.X),
                   deps=[g, LU.get("t16")])
        bk.read(2, trm)
        tkm = p.do("vector", lambda e: e.tensor_max(kmax_c[:], kmax_c[:], t16[:]),
                   deps=[trm, t_kin])
        LU["t16"] = tkm
        LU["big"] = trm

        for c in range(4):
            ev_t = []
            for oh in range(2):
                bkid = 4 + oh
                mms = []
                for cc in range(4):
                    mms.append(((lambda cc=cc, c=c: ckvT_t[:, 512 * cc + 128 * c:
                                                           512 * cc + 128 * c + 128]),
                                (lambda cc=cc, oh=oh: WC("wuv", cc, 512 * oh, 512 * oh + 512)),
                                [tw_uv[-1], ckv_evac[-1]], False, None))
                mms.append(((lambda: ones8[0:1, :]),
                            (lambda oh=oh: brow["buv"][:, 512 * oh:512 * oh + 512]),
                            [t_ones, t_b["buv"]], False, None))
                g = mm_group(bkid, mms, extra_deps=[LU.get(("vev", c, oh))])
                for hh in range(8):
                    hg = 8 * oh + hh
                    tv = p.do("vector", lambda e, c=c, hh=hh, hg=hg, bkid=bkid: e.tensor_copy(
                        v_t[:, 1040 * c + 65 * hg: 1040 * c + 65 * hg + 64],
                        PB(bkid)[:, 64 * hh:64 * hh + 64]),
                        deps=[g, t_vones, LU.get(("vsp", c)), LU.get(("vdp", c))])
                    bk.read(bkid, tv)
                    ev_t.append(tv)
                LU[("vev", c, oh)] = ev_t[-1]
            tsp = p.do("sync", lambda e, c=c, r0_tile=r0_tile: e.dma_start(
                out=v_hbm[r0_tile + 128 * c: r0_tile + 128 * c + 128, :],
                in_=v_t[:, 1040 * c:1040 * c + 1040]), deps=ev_t, dma=f"vsp{c}")
            LU[("vsp", c)] = tsp
            if d_v is not None:
                LU[("vdp", c)] = p.do("sync", lambda e, c=c, r0_tile=r0_tile: e.dma_start(
                    out=d_v[r0_tile + 128 * c: r0_tile + 128 * c + 128, :],
                    in_=v_t[:, 1040 * c:1040 * c + 1040]), deps=ev_t, dma=f"dpv{c}")

    p.barrier()
    wa1.close()

    wa2 = contextlib.ExitStack()
    load_brow(wa2, "bqr", "biasA2")
    tw_dq = load_w(wa2, "wdq", D, DQ)
    tw_uq = load_w(wa2, "wuq", DQ, D)
    tw_qr = load_w(wa2, "wqr", DQ, D)

    tqm = None
    for stq in range(4):
        r0 = 512 * stq
        xnT_toks = []
        for j in range(8):
            xnT_toks.append(p.do("sync", lambda e, j=j, r0=r0: e.dma_start(
                out=xnT_t[:, 512 * j:512 * j + 512],
                in_=xnT_hbm[128 * j:128 * j + 128, r0:r0 + 512]),
                deps=[LU.get("xnT_t")], dma="xnT_rl"))

        cq_evac = []
        for oc in range(4):
            bkid = 2 + (oc % 2)
            mms = []
            for ic in range(8):
                mms.append(((lambda ic=ic, oc=oc: WC("wdq", ic, 128 * oc, 128 * oc + 128)),
                            (lambda ic=ic: xnT_t[:, 512 * ic:512 * ic + 512]),
                            [tw_dq[-1], xnT_toks[-1]], False, None))
            g = mm_group(bkid, mms)
            tv = p.do("scalar", lambda e, oc=oc, bkid=bkid: e.activation(
                ckvT_t[:, 512 * oc:512 * oc + 512], PB(bkid)[:, 0:512], AF.Identity,
                bias=bcol["bdq"][:, oc:oc + 1], scale=1.0),
                deps=[g, t_b["bdq"], LU.get("ckvT")])
            bk.read(bkid, tv)
            cq_evac.append(tv)
            LU["xnT_t"] = g
        LU["ckvT"] = cq_evac[-1]

        rope_done = []
        for c in range(4):
            tcos, tsin = cos_sin(r0 + 128 * c)
            for oh in range(2):
                bkid = 4 + oh
                mms = []
                for cc in range(4):
                    mms.append(((lambda cc=cc, c=c: ckvT_t[:, 512 * cc + 128 * c:
                                                           512 * cc + 128 * c + 128]),
                                (lambda cc=cc, oh=oh: WC("wqr", cc, 512 * oh, 512 * oh + 512)),
                                [tw_qr[-1], cq_evac[-1]], False, None))
                mms.append(((lambda: ones8[0:1, :]),
                            (lambda oh=oh: brow["bqr"][:, 512 * oh:512 * oh + 512]),
                            [t_ones, t_b["bqr"]], False, None))
                g = mm_group(bkid, mms, extra_deps=[LU.get(("krp", c, oh))])
                dst = big_t[:, 1024 * c + 512 * oh: 1024 * c + 512 * oh + 512]
                t6, readers = rope_apply(PB(bkid)[:, 0:512], oh, dst, g, tcos, tsin)
                for rt in readers + [t6]:
                    bk.read(bkid, rt)
                LU[("krp", c, oh)] = t6
                rope_done.append(t6)

        qT_evac = []
        for oc in range(8):
            bkid = 6 + (oc % 2)
            mms = []
            for cc in range(4):
                mms.append(((lambda cc=cc, oc=oc: WC("wuq", cc, 128 * oc, 128 * oc + 128)),
                            (lambda cc=cc: ckvT_t[:, 512 * cc:512 * cc + 512]),
                            [tw_uq[-1], cq_evac[-1]], False, None))
            for c in range(4):
                mms.append(((lambda c=c, oc=oc: big_t[:, 1024 * c + 128 * oc:
                                                      1024 * c + 128 * oc + 128]),
                            (lambda: ident[:]),
                            [rope_done[-1], t_id], True, PB(bkid)[:, 128 * c:128 * c + 128]))
            g = mm_group(bkid, mms)
            tv = p.do("scalar", lambda e, oc=oc, bkid=bkid: e.activation(
                outT_t[:, 512 * oc:512 * oc + 512], PB(bkid)[:, 0:512], AF.Identity,
                bias=buqs[:, oc:oc + 1], scale=INV_SCALE),
                deps=[g, t_buqs, LU.get(("outT", oc)), LU.get(("outT_dp", oc))])
            bk.read(bkid, tv)
            tsp = p.do("sync", lambda e, oc=oc, r0=r0: e.dma_start(
                out=qT_hbm[128 * oc:128 * oc + 128, r0:r0 + 512],
                in_=outT_t[:, 512 * oc:512 * oc + 512]), deps=[tv], dma=f"osp{oc}")
            LU[("outT", oc)] = tsp
            if d_qT is not None:
                LU[("outT_dp", oc)] = p.do("sync", lambda e, oc=oc, r0=r0: e.dma_start(
                    out=d_qT[128 * oc:128 * oc + 128, r0:r0 + 512],
                    in_=outT_t[:, 512 * oc:512 * oc + 512]), deps=[tv], dma=f"dpk{oc}")
            qT_evac.append(tv)

        tsq = p.do("scalar", lambda e: e.activation(big_t[:], outT_t[:], AF.Square,
                                                    bias=0.0, scale=1.0),
                   deps=qT_evac + [rope_done[-1], LU.get("big")])
        mms = []
        for ic in range(8):
            mms.append(((lambda ic=ic: e_sb[:, 16 * ic:16 * ic + 16]),
                        (lambda ic=ic: big_t[:, 512 * ic:512 * ic + 512]),
                        [tsq], False, PB(2)[0:16, 0:512]))
        g = mm_group(2, mms)
        trm = p.do("vector", lambda e: e.reduce_max(t16[:], PB(2)[0:16, 0:512], AX.X),
                   deps=[g, LU.get("t16")])
        bk.read(2, trm)
        tqm = p.do("vector", lambda e: e.tensor_max(qmax_c[:], qmax_c[:], t16[:]),
                   deps=[trm, t_qin])
        LU["t16"] = tqm
        LU["big"] = trm

    g1 = mm_group(3, [((lambda: kmax_c[:]), (lambda: ident[0:16, 0:16]),
                       [tkm, tqm, t_id], True, PB(3)[0:1, 0:16])])
    tkr = p.do("vector", lambda e: e.tensor_copy(mrow[:, 0:16], PB(3)[0:1, 0:16]), deps=[g1])
    bk.read(3, tkr)
    g2 = mm_group(3, [((lambda: qmax_c[:]), (lambda: ident[0:16, 0:16]),
                       [tqm, t_id], True, PB(3)[0:1, 0:16])])
    tqr2 = p.do("vector", lambda e: e.tensor_copy(mrow[:, 16:32], PB(3)[0:1, 0:16]), deps=[g2])
    bk.read(3, tqr2)
    tpr = p.do("vector", lambda e: e.tensor_tensor(
        mrow[:, 0:16], mrow[:, 0:16], mrow[:, 16:32], Op.mult), deps=[tkr, tqr2])
    tm2 = p.do("vector", lambda e: e.reduce_max(mrow[:, 32:33], mrow[:, 0:16], AX.X), deps=[tpr])
    tm1 = p.do("scalar", lambda e: e.activation(mrow[:, 32:33], mrow[:, 32:33], AF.Sqrt,
                                                bias=0.0, scale=1.0), deps=[tm2])
    tmn = p.do("vector", lambda e: e.tensor_scalar_mul(mrow[:, 32:33], mrow[:, 32:33], -1.0),
               deps=[tm1])
    g3 = mm_group(3, [((lambda: ones8[0:1, :]), (lambda: mrow[:, 32:33]),
                       [tmn, t_ones], False, PB(3)[:, 0:1])])
    t_negm = p.do("vector", lambda e: e.tensor_copy(negM[:], PB(3)[:, 0:1]), deps=[g3])
    bk.read(3, t_negm)
    if d_negm is not None:
        p.do("sync", lambda e: e.dma_start(out=d_negm[:, :], in_=negM[0:1, :]),
             deps=[t_negm], dma="dump")

    p.barrier()
    wa2.close()
    astack.close()

    cdstack = contextlib.ExitStack()
    xn2T_sb = mk_sb(cdstack)("xn2T_sb", [128, 8 * 2048])
    bcstack = contextlib.ExitStack()
    OT_sb = mk_sb(bcstack)("OT_sb", [128, 8 * 2048])
    bstack = contextlib.ExitStack()
    bsb = mk_sb(bstack)
    qt_sb = bsb("qt_sb", [128, 2 * 2048])
    kt_sb = [bsb(f"kt_sb{i}", [128, 256]) for i in range(2)]
    vt_sb = [bsb(f"vt_sb{i}", [128, 260]) for i in range(2)]
    pt_sb = bsb("pt_sb", [128, 4 * 512])
    onp = bsb("onp", [128, 256])
    lrec = bsb("lrec", [128, 4])

    tev = None
    for Q in range(4):
        qload = []
        for hp in range(2):
            qload.append(p.do("sync", lambda e, Q=Q, hp=hp: e.dma_start(
                out=qt_sb[:, 2048 * hp:2048 * hp + 2048],
                in_=qT_hbm[256 * Q + 128 * hp:256 * Q + 128 * hp + 128, :]),
                deps=[LU.get("qt_sb")], dma=f"qld{hp}"))
        for qt in range(4):
            o_first = [True] * 4
            o_group_tok = [None] * 4
            sc_rot = 0
            for kc in range(32):
                bufi = kc % 2
                tk = p.do("sync", lambda e, Q=Q, kc=kc, bufi=bufi: e.dma_start(
                    out=kt_sb[bufi][:].rearrange("p (a k) -> p a k", a=2),
                    in_=kT_hbm[256 * Q:256 * Q + 256, 128 * kc:128 * kc + 128]
                        .rearrange("(a p) k -> p a k", p=128)),
                    deps=[LU.get(("kt", bufi))], dma=f"kt{bufi}")
                tvl = p.do("sync", lambda e, Q=Q, kc=kc, bufi=bufi: e.dma_start(
                    out=vt_sb[bufi][:],
                    in_=v_hbm[128 * kc:128 * kc + 128, 260 * Q:260 * Q + 260]),
                    deps=[LU.get(("vt", bufi))], dma=f"vt{bufi}")
                kt_last = vt_last = None
                for hp in range(2):
                    sb0 = 4 + (sc_rot % 3)
                    sb1 = 4 + ((sc_rot + 1) % 3)
                    sc_rot += 2
                    tA = p.do("tensor", lambda e, hp=hp, qt=qt, bufi=bufi, sb0=sb0: e.matmul(
                        PB(sb0)[:, 0:512],
                        kt_sb[bufi][0:64, 128 * hp:128 * hp + 128],
                        qt_sb[0:64, 2048 * hp + 512 * qt: 2048 * hp + 512 * qt + 512],
                        start=True, stop=True),
                        deps=[tk, qload[0], qload[1]] + bk.pre_write(sb0))
                    bk.wrote(sb0, tA)
                    tB = p.do("tensor", lambda e, hp=hp, qt=qt, bufi=bufi, sb1=sb1: e.matmul(
                        PB(sb1)[:, 0:512],
                        kt_sb[bufi][64:128, 128 * hp:128 * hp + 128],
                        qt_sb[64:128, 2048 * hp + 512 * qt: 2048 * hp + 512 * qt + 512],
                        start=True, stop=True, tile_position=(64, 0)),
                        deps=[tk, qload[0], qload[1]] + bk.pre_write(sb1))
                    bk.wrote(sb1, tB)
                    kt_last = tB
                    for ab in range(2):
                        hh = 2 * hp + ab
                        sbb = sb0 if ab == 0 else sb1
                        gtok = tA if ab == 0 else tB
                        slot = hh
                        tex = p.do("scalar", lambda e, sbb=sbb, slot=slot: e.activation(
                            pt_sb[:, 512 * slot:512 * slot + 512], PB(sbb)[:, 0:512],
                            AF.Exp, bias=negM[:], scale=1.0),
                            deps=[gtok, t_negm, LU.get(("pt", slot))])
                        bk.read(sbb, tex)
                        is_last = (kc == 31 and hp == 1 and ab == 1)
                        av_last = None
                        for qs in range(4):
                            obk = qs
                            first = o_first[qs]
                            o_first[qs] = False
                            dd = [tex, tvl]
                            if first:
                                dd += bk.pre_write(obk)
                            tav = p.do("tensor", lambda e, qs=qs, hh=hh, slot=slot, bufi=bufi,
                                       first=first, is_last=is_last, obk=obk: e.matmul(
                                PB(obk)[:, 65 * hh:65 * hh + 65],
                                pt_sb[:, 512 * slot + 128 * qs: 512 * slot + 128 * qs + 128],
                                vt_sb[bufi][:, 65 * hh:65 * hh + 65],
                                start=first, stop=is_last), deps=dd)
                            if first:
                                bk.wrote(obk, tav)
                            o_group_tok[qs] = tav
                            av_last = tav
                        LU[("pt", slot)] = av_last
                        vt_last = av_last
                LU[("kt", bufi)] = kt_last
                LU[("vt", bufi)] = vt_last
            for qs in range(4):
                obk = qs
                trc = p.do("vector", lambda e, obk=obk: e.reciprocal(
                    lrec[:], PB(obk)[:, 64:64 + 3 * 65 + 1:65]),
                    deps=[o_group_tok[qs], LU.get("lrec")])
                bk.read(obk, trc)
                tn_last = None
                for hh in range(4):
                    tn = p.do("vector", lambda e, obk=obk, hh=hh: e.tensor_scalar(
                        onp[:, 64 * hh:64 * hh + 64],
                        PB(obk)[:, 65 * hh:65 * hh + 64], lrec[:, hh:hh + 1], None, Op.mult),
                        deps=[trc, LU.get("onp")])
                    bk.read(obk, tn)
                    tn_last = tn
                LU["lrec"] = tn_last
                for dj in range(2):
                    tbk = 4 + dj
                    mms = [((lambda dj=dj: onp[:, 128 * dj:128 * dj + 128]),
                            (lambda: ident[:]),
                            [tn_last, t_id], True, PB(tbk)[:, 0:128])]
                    g = mm_group(tbk, mms)
                    fc = 2 * Q + dj
                    tev = p.do("vector", lambda e, fc=fc, qt=qt, qs=qs, tbk=tbk: e.tensor_copy(
                        OT_sb[:, 2048 * fc + 512 * qt + 128 * qs:
                              2048 * fc + 512 * qt + 128 * qs + 128],
                        PB(tbk)[:, 0:128]), deps=[g])
                    bk.read(tbk, tev)
                    LU["onp"] = tev
        LU["qt_sb"] = tev

    if d_OT is not None:
        for j in range(8):
            p.do("sync", lambda e, j=j: e.dma_start(
                out=d_OT[128 * j:128 * j + 128, :], in_=OT_sb[:, 2048 * j:2048 * j + 2048]),
                deps=[tev], dma="dump")

    p.barrier()
    bstack.close()

    wcstack = contextlib.ExitStack()
    load_brow(wcstack, "bo", "biasC")
    tw_o = load_w(wcstack, "wo", D, D)

    for qc in range(16):
        ao_toks = []
        for oh in range(2):
            bkid = oh
            mms = []
            for dc in range(8):
                mms.append(((lambda dc=dc, qc=qc: OT_sb[:, 2048 * dc + 128 * qc:
                                                        2048 * dc + 128 * qc + 128]),
                            (lambda dc=dc, oh=oh: WC("wo", dc, 512 * oh, 512 * oh + 512)),
                            [tw_o[-1]], False, None))
            mms.append(((lambda: ones8[0:1, :]),
                        (lambda oh=oh: brow["bo"][:, 512 * oh:512 * oh + 512]),
                        [t_ones, t_b["bo"]], False, None))
            g = mm_group(bkid, mms)
            tv = p.do("scalar", lambda e, oh=oh, bkid=bkid: e.activation(
                ao_t[:, 512 * oh:512 * oh + 512], PB(bkid)[:, 0:512], AF.Identity,
                bias=0.0, scale=1.0), deps=[g, LU.get("ao")])
            bk.read(bkid, tv)
            ao_toks.append(tv)
        LU["ao"] = ao_toks[-1]
        ts = ln_stats(ao_toks[-1], ao_t[:], dep_extra=[ao_toks[0]])
        tln = p.do("scalar", lambda e: e.activation(
            lnout[:], ao_t[:], AF.Identity, bias=ln_m[:], scale=ln_c[3][:]),
            deps=[ts, ao_toks[0], ao_toks[1], LU.get("lnout")])
        LU["ao"] = tln
        tl = p.do("sync", lambda e, qc=qc: e.dma_start(
            out=xt[0][:], in_=xv[128 * qc:128 * qc + 128, :]),
            deps=[LU.get(("xt", 0))], dma="xt0")
        tres = p.do("vector", lambda e: e.tensor_tensor(res_t[:], xt[0][:], lnout[:], Op.add),
                    deps=[tl, tln, LU.get("res_t"), LU.get("res_dump"), LU.get("res_sp_last")])
        LU[("xt", 0)] = tres
        LU["lnout"] = tres
        LU["res_sp_last"] = p.do("sync", lambda e, qc=qc: e.dma_start(
            out=res_hbm[128 * qc:128 * qc + 128, :], in_=res_t[:]), deps=[tres], dma="res_sp")
        if d_res is not None:
            LU["res_dump"] = p.do("sync", lambda e, qc=qc: e.dma_start(
                out=d_res[128 * qc:128 * qc + 128, :], in_=res_t[:]), deps=[tres], dma="dpr")
        ts2 = ln_stats(tres, res_t[:])
        txn2 = p.do("scalar", lambda e: e.activation(
            xn[:], res_t[:], AF.Identity, bias=ln_m[:], scale=ln_c[3][:]),
            deps=[ts2, LU.get("xn")])
        LU["res_t"] = txn2
        ev_last = None
        for half in range(2):
            bkid = 2 + half
            mms = []
            for jj in range(4):
                j = 4 * half + jj
                mms.append(((lambda j=j: xn[:, 128 * j:128 * j + 128]),
                            (lambda: ident[:]),
                            [txn2, t_id], True, PB(bkid)[:, 128 * jj:128 * jj + 128]))
            g = mm_group(bkid, mms)
            for jj in range(4):
                j = 4 * half + jj
                tv = p.do("vector", lambda e, j=j, jj=jj, qc=qc, bkid=bkid: e.tensor_copy(
                    xn2T_sb[:, 2048 * j + 128 * qc: 2048 * j + 128 * qc + 128],
                    PB(bkid)[:, 128 * jj:128 * jj + 128]), deps=[g])
                bk.read(bkid, tv)
                ev_last = tv
        LU["xn"] = ev_last

    p.barrier()
    wcstack.close()
    bcstack.close()

    dstack = contextlib.ExitStack()
    dsb = mk_sb(dstack)
    wr_raw = dsb("wr_raw", [8, 1024])
    wrT_sb = dsb("wrT_sb", [128, 8 * 8])
    ez_sb = dsb("ez_sb", [128, 8])
    zcol = dsb("zcol", [128, 1])
    y_acc = dsb("y_acc", [128, 16 * 1024])
    w1_sb = [dsb(f"w1_sb{i}", [128, 8 * 256]) for i in range(2)]
    w2_sb = [dsb(f"w2_sb{i}", [128, 2 * 1024]) for i in range(2)]
    h_sb = dsb("h_sb", [128, 2 * 512])

    load_brow(dstack, "b2", "biasD")
    twr = p.do("gpsimd", lambda e: e.dma_start(out=wr_raw[:], in_=win["wr"][:, :]), dma="wr")
    t_wrT = None
    for ic in range(8):
        g = mm_group(4, [((lambda ic=ic: wr_raw[0:8, 128 * ic:128 * ic + 128]),
                          (lambda: ident[0:8, 0:8]),
                          [twr, t_id], True, PB(4)[:, 0:8])])
        t_wrT = p.do("vector", lambda e, ic=ic: e.tensor_copy(
            wrT_sb[:, 8 * ic:8 * ic + 8], PB(4)[:, 0:8]), deps=[g])
        bk.read(4, t_wrT)

    for qc in range(16):
        mms = []
        for ic in range(8):
            mms.append(((lambda ic=ic, qc=qc: xn2T_sb[:, 2048 * ic + 128 * qc:
                                                      2048 * ic + 128 * qc + 128]),
                        (lambda ic=ic: wrT_sb[:, 8 * ic:8 * ic + 8]),
                        [t_wrT], False, PB(4)[:, 0:8]))
        g = mm_group(4, mms)
        tmx = p.do("vector", lambda e: e.reduce_max(zcol[:], PB(4)[:, 0:8], AX.X, negate=True),
                   deps=[g, LU.get("zcol")])
        bk.read(4, tmx)
        tex = p.do("scalar", lambda e: e.activation(
            ez_sb[:], PB(4)[:, 0:8], AF.Exp, bias=zcol[:], scale=1.0, accum_out=zcol[:]),
            deps=[tmx, LU.get("ez")])
        bk.read(4, tex)
        trc = p.do("vector", lambda e: e.reciprocal(zcol[:], zcol[:]), deps=[tex])
        tp7 = p.do("vector", lambda e, qc=qc: e.tensor_tensor(
            p7_sb[:, qc:qc + 1], ez_sb[:, 7:8], zcol[:], Op.mult), deps=[trc])
        LU["zcol"] = tp7
        LU["ez"] = tp7

    tz = p.do("vector", lambda e: e.memset(y_acc[:], 0.0))
    yacc_tok = {qc: tz for qc in range(16)}

    for f in range(16):
        bufi = f % 2
        tw1 = []
        for ic in range(8):
            tw1.append(p.do("gpsimd", lambda e, f=f, ic=ic, bufi=bufi: e.dma_start(
                out=w1_sb[bufi][:, 256 * ic:256 * ic + 256],
                in_=win["w1"][128 * ic:128 * ic + 128, 256 * f:256 * f + 256]),
                deps=[LU.get(("w1", bufi))], dma=f"w1_{bufi}"))
        tw2 = []
        for fs in range(2):
            tw2.append(p.do("gpsimd", lambda e, f=f, fs=fs, bufi=bufi: e.dma_start(
                out=w2_sb[bufi][:, 1024 * fs:1024 * fs + 1024],
                in_=win["w2"][256 * f + 128 * fs:256 * f + 128 * fs + 128, :]),
                deps=[LU.get(("w2", bufi))], dma=f"w2_{bufi}"))
        ylast = None
        for qt in range(4):
            h_toks = []
            for fs in range(2):
                bkid = 4 + fs
                mms = []
                for ic in range(8):
                    mms.append(((lambda ic=ic, fs=fs, bufi=bufi: w1_sb[bufi][
                                    :, 256 * ic + 128 * fs: 256 * ic + 128 * fs + 128]),
                                (lambda ic=ic, qt=qt: xn2T_sb[:, 2048 * ic + 512 * qt:
                                                              2048 * ic + 512 * qt + 512]),
                                [tw1[-1]], False, None))
                g = mm_group(bkid, mms)
                th = p.do("scalar", lambda e, fs=fs, f=f, bkid=bkid: e.activation(
                    h_sb[:, 512 * fs:512 * fs + 512], PB(bkid)[:, 0:512], AF.Gelu,
                    bias=bcol["b1"][:, 2 * f + fs:2 * f + fs + 1], scale=1.0),
                    deps=[g, t_b["b1"], LU.get(("h", fs))])
                bk.read(bkid, th)
                h_toks.append(th)
            for qs in range(4):
                qc = 4 * qt + qs
                for oh in range(2):
                    bkid = 2 * (qs % 2) + oh
                    mms = []
                    for fs in range(2):
                        mms.append(((lambda fs=fs, qs=qs: h_sb[:, 512 * fs + 128 * qs:
                                                               512 * fs + 128 * qs + 128]),
                                    (lambda fs=fs, oh=oh, bufi=bufi: w2_sb[bufi][
                                        :, 1024 * fs + 512 * oh: 1024 * fs + 512 * oh + 512]),
                                    [h_toks[fs], tw2[-1]], False, None))
                    if f == 15:
                        mms.append(((lambda: ones8[0:1, :]),
                                    (lambda oh=oh: brow["b2"][:, 512 * oh:512 * oh + 512]),
                                    [t_ones, t_b["b2"]], False, None))
                    g = mm_group(bkid, mms)
                    ty = p.do("vector", lambda e, qc=qc, oh=oh, bkid=bkid: e.tensor_tensor(
                        y_acc[:, 1024 * qc + 512 * oh: 1024 * qc + 512 * oh + 512],
                        y_acc[:, 1024 * qc + 512 * oh: 1024 * qc + 512 * oh + 512],
                        PB(bkid)[:, 0:512], Op.add),
                        deps=[g, yacc_tok[qc]])
                    bk.read(bkid, ty)
                    yacc_tok[qc] = ty
                    ylast = ty
            LU[("h", 0)] = ylast
            LU[("h", 1)] = ylast
        LU[("w1", bufi)] = ylast
        LU[("w2", bufi)] = ylast

    for qc in range(16):
        tyb = p.do("vector", lambda e, qc=qc: e.tensor_scalar(
            ao_t[:], y_acc[:, 1024 * qc:1024 * qc + 1024], p7_sb[:, qc:qc + 1], None, Op.mult),
            deps=[yacc_tok[qc], LU.get("ao")])
        LU["ao"] = tyb
        ts = ln_stats(tyb, ao_t[:])
        tln = p.do("scalar", lambda e: e.activation(
            lnout[:], ao_t[:], AF.Identity, bias=ln_m[:], scale=ln_c[3][:]),
            deps=[ts, tyb, LU.get("lnout")])
        LU["ao"] = tln
        trl = p.do("sync", lambda e, qc=qc: e.dma_start(
            out=res_t[:], in_=res_hbm[128 * qc:128 * qc + 128, :]),
            deps=[LU.get("res_t"), LU.get("out_last")], dma="res_rl")
        tout = p.do("vector", lambda e: e.tensor_tensor(res_t[:], res_t[:], lnout[:], Op.add),
                    deps=[trl, tln])
        LU["res_t"] = tout
        LU["lnout"] = tout
        LU["out_last"] = p.do("sync", lambda e, qc=qc: e.dma_start(
            out=out_hbm[128 * qc:128 * qc + 128, :], in_=res_t[:]), deps=[tout], dma="out")

    dstack.close()
    cdstack.close()

    with gstack:
        with nc.Block() as block:
            p.emit(nc, gstack, block, csem)
    return nc


_cache = {}


def kernel(**kw):
    x = np.asarray(kw["L0_attn_x"], np.float32)
    if "nc" not in _cache:
        _cache["nc"] = build_program()
    nc = _cache["nc"]
    shared = {
        "wdq": kw["L0_attn_Wd_q_w"], "wuq": kw["L0_attn_Wu_q_w"],
        "wqr": kw["L0_attn_Wq_r_w"], "wkr": kw["L0_attn_Wk_r_w"],
        "wdkv": kw["L0_attn_Wd_kv_w"], "wuk": kw["L0_attn_Wu_k_w"],
        "wuv": kw["L0_attn_Wu_v_w"], "wo": kw["L0_attn_Wo_w"],
        "wr": kw["L0_ffn_Wr"], "w1": kw["L0_ffn_W1"][7], "w2": kw["L0_ffn_W2"][7],
        "bdq": kw["L0_attn_Wd_q_b"], "buq": kw["L0_attn_Wu_q_b"],
        "bqr": kw["L0_attn_Wq_r_b"], "bkr": kw["L0_attn_Wk_r_b"],
        "bdkv": kw["L0_attn_Wd_kv_b"], "buk": kw["L0_attn_Wu_k_b"],
        "buv": kw["L0_attn_Wu_v_b"], "bo": kw["L0_attn_Wo_b"],
        "b1": kw["L0_ffn_b1"][7], "b2": kw["L0_ffn_b2"][7],
    }
    shared = {k: np.ascontiguousarray(np.asarray(v, np.float32)) for k, v in shared.items()}
    in_maps = []
    for core in range(8):
        b, hf = core // 2, core % 2
        q0 = NQ * hf
        xvv = np.ascontiguousarray(np.concatenate([x[b, q0:], x[b, :q0]], axis=0))
        m = dict(shared)
        m["xv"] = xvv
        m["seg"] = np.array([q0, (q0 + NQ) % S], np.float32)
        in_maps.append(m)
    res = run_bass_kernel_spmd(nc, in_maps, core_ids=list(range(8)))
    r0 = np.asarray(res.results[0]["out"])
    out = np.empty((B, S, D), r0.dtype)
    for core in range(8):
        b, hf = core // 2, core % 2
        out[b, NQ * hf:NQ * hf + NQ] = np.asarray(res.results[core]["out"])
    return out


# version 04
# speedup vs baseline: 1.1698x; 1.1698x over previous
import contextlib
import math
import numpy as np
import concourse.bass as bass
import concourse.mybir as mybir
from concourse.bass_utils import run_bass_kernel_spmd
from concourse.alu_op_type import AluOpType as Op

F32 = mybir.dt.float32
BF16 = mybir.dt.bfloat16
AF = mybir.ActivationFunctionType
AX = mybir.AxisListType

B, S, D = 4, 4096, 1024
NQ = 2048
H = 16
DQ = DKV = 512
FF = 4096
EPS = 1e-5
SCALE = float(np.sqrt(np.float32(H + DQ + DKV)))
INV_SCALE = 1.0 / SCALE
PI = math.pi
TWO_PI = 2 * math.pi
HALF_PI = 0.5 * math.pi
HALF_PI_F32 = float(np.float32(0.5 * math.pi))
HALF_PI_LO = HALF_PI - HALF_PI_F32
LN1E4 = 9.210340371976184

ENGS = ("sync", "tensor", "vector", "scalar", "gpsimd")


class Prog:

    def __init__(self):
        self.steps = []
        self.barrier_toks = []

    def do(self, eng, fn, deps=(), dma=None):
        t = len(self.steps)
        dd = [d for d in deps if d is not None]
        dd.extend(self.barrier_toks)
        self.steps.append((eng, fn, tuple(dd), dma))
        return t

    def barrier(self):
        last = {}
        for i, (eng, _f, _d, dma) in enumerate(self.steps):
            last[eng if dma is None else ("ch", dma)] = i
        self.barrier_toks = list(last.values())

    def emit(self, nc, gstack, block, csem):
        steps = self.steps
        need = [False] * len(steps)
        for (_e, _f, deps, _dma) in steps:
            for d in deps:
                need[d] = True
        chans = []
        for (_e, _f, _d, dma) in steps:
            if dma is not None and dma not in chans:
                chans.append(dma)
        chsem = {c: gstack.enter_context(nc.semaphore("ch_" + c)) for c in chans}
        meta = [None] * len(steps)
        ccount = {e: 0 for e in ENGS}
        chcount = {c: 0 for c in chans}
        eng_chfinal = {e: {} for e in ENGS}
        for i, (eng, _f, _deps, dma) in enumerate(steps):
            if dma is not None:
                chcount[dma] += 16
                meta[i] = ("ch", dma, chcount[dma])
                eng_chfinal[eng][dma] = chcount[dma]
            else:
                if need[i]:
                    ccount[eng] += 1
                meta[i] = ("c", eng, ccount[eng])

        def body_for(E):
            mine = [(i, steps[i]) for i in range(len(steps)) if steps[i][0] == E]

            def body(engobj):
                ensured = {}
                for i, (_e, fn, deps, dma) in mine:
                    for d in deps:
                        kind, key2, dval = meta[d]
                        key = (kind, key2)
                        if ensured.get(key, 0) >= dval:
                            continue
                        sem = csem[key2] if kind == "c" else chsem[key2]
                        engobj.wait_ge(sem, dval)
                        ensured[key] = dval
                    inst = fn(engobj)
                    if dma is not None:
                        inst.then_inc(chsem[dma], 16)
                    elif need[i]:
                        inst.then_inc(csem[E], 1)
                for c, v in eng_chfinal[E].items():
                    engobj.wait_ge(chsem[c], v)

            return body

        for E in ENGS:
            getattr(block, E)(body_for(E))


class Banks:
    def __init__(self):
        self.readers = {b: [] for b in range(8)}
        self.writer = {b: None for b in range(8)}

    def pre_write(self, b):
        r = list(self.readers[b])
        if self.writer[b] is not None:
            r.append(self.writer[b])
        return r

    def wrote(self, b, tok):
        self.writer[b] = tok
        self.readers[b] = []

    def read(self, b, tok):
        self.readers[b].append(tok)


def build_program(dump=()):
    nc = bass.Bass("TRN2", target_bir_lowering=False)
    p = Prog()
    bk = Banks()
    LU = {}

    xv = nc.dram_tensor("xv", [S, D], F32, kind="ExternalInput").ap()
    seg = nc.dram_tensor("seg", [2], F32, kind="ExternalInput").ap()
    win = {}
    for nm, shp in [("wdq", [D, DQ]), ("wuq", [DQ, D]), ("wqr", [DQ, D]),
                    ("wkr", [D, D]), ("wdkv", [D, DKV]), ("wuk", [DKV, D]),
                    ("wuv", [DKV, D]), ("wo", [D, D]), ("wr", [8, D]),
                    ("w1", [D, FF]), ("w2", [FF, D]),
                    ("bdq", [DQ]), ("buq", [D]), ("bqr", [D]), ("bkr", [D]),
                    ("bdkv", [DKV]), ("buk", [D]), ("buv", [D]), ("bo", [D]),
                    ("b1", [FF]), ("b2", [D])]:
        win[nm] = nc.dram_tensor(nm, shp, F32, kind="ExternalInput").ap()
    out_hbm = nc.dram_tensor("out", [NQ, D], F32, kind="ExternalOutput").ap()

    dumps = {}

    def dump_t(nm, shp):
        if nm in dump:
            dumps[nm] = nc.dram_tensor("dump_" + nm, shp, F32, kind="ExternalOutput").ap()
        return dumps.get(nm)

    xnT_hbm = nc.dram_tensor("xnT_hbm", [D, S], F32, kind="Internal").ap()
    kT_hbm = nc.dram_tensor("kT_hbm", [D, S], F32, kind="Internal").ap()
    v_hbm = nc.dram_tensor("v_hbm", [S, H * 65], F32, kind="Internal").ap()
    qT_hbm = nc.dram_tensor("qT_hbm", [D, NQ], F32, kind="Internal").ap()
    res_hbm = nc.dram_tensor("res_hbm", [NQ, D], F32, kind="Internal").ap()

    d_xnT = dump_t("xnT", [D, S])
    d_kT = dump_t("kT", [D, S])
    d_v = dump_t("v", [S, H * 65])
    d_qT = dump_t("qT", [D, NQ])
    d_OT = dump_t("OT", [D, NQ])
    d_res = dump_t("res", [NQ, D])
    d_negm = dump_t("negm", [1, 1])

    gstack = contextlib.ExitStack()

    def mk_sb(stk):
        return lambda nm, shp, dt=F32: stk.enter_context(nc.sbuf_tensor(nm, shp, dt)).ap()

    sb = mk_sb(gstack)

    ident = sb("ident", [128, 128])
    ones8 = sb("ones8", [8, 128])
    theta_bc = sb("theta_bc", [128, 512])
    halfpi_col = sb("halfpi_col", [128, 1])
    segcol = sb("segcol", [128, 2])
    seg_sb = sb("seg_sb", [1, 2])
    negM = sb("negM", [128, 1])
    e_sb = sb("e_sb", [128, 8 * 16])
    kmax_c = sb("kmax_c", [16, 1])
    qmax_c = sb("qmax_c", [16, 1])
    t16 = sb("t16", [16, 1])
    p7_sb = sb("p7_sb", [128, 16])
    bcol = {}
    for nm, w in [("bdq", 4), ("bdkv", 4), ("buq", 8), ("buk", 8), ("b1", 32)]:
        bcol[nm] = sb(nm + "_col", [128, w])
    buqs = sb("buqs_col", [128, 8])
    ln_c = [sb(f"ln_c{i}", [128, 1]) for i in range(4)]
    ln_m = sb("ln_m", [128, 1])
    sq_scr = sb("sq_scr", [128, 1024])
    lnout = sb("lnout", [128, 1024])
    mrow = sb("mrow", [1, 64])
    xt = [sb(f"xt{i}", [128, 1024]) for i in range(2)]
    xn = sb("xn", [128, 1024])
    res_t = sb("res_t", [128, 1024])
    ao_t = sb("ao_t", [128, 1024])

    psum = gstack.enter_context(nc.psum_tensor("pall", [128, 4096], F32)).ap()
    PB = lambda b: psum[:, 512 * b:512 * (b + 1)]

    csem = {e: gstack.enter_context(nc.semaphore("c_" + e)) for e in ENGS}

    t_id0 = p.do("gpsimd", lambda e: e.memset(ident[:], 0.0))
    t_id = p.do("gpsimd", lambda e: e.affine_select(
        out=ident[:], in_=ident[:], compare_op=Op.not_equal, fill=1.0,
        base=0, pattern=[[-1, 128]], channel_multiplier=1), deps=[t_id0])
    t_ones = p.do("gpsimd", lambda e: e.memset(ones8[:], 1.0))
    t_hp0 = p.do("gpsimd", lambda e: e.memset(halfpi_col[:], HALF_PI_F32))
    t_hp = p.do("vector", lambda e: e.tensor_scalar_add(halfpi_col[:], halfpi_col[:],
                                                        HALF_PI_LO), deps=[t_hp0])
    t_kin = p.do("gpsimd", lambda e: e.memset(kmax_c[:], -3.0e38))
    t_qin = p.do("gpsimd", lambda e: e.memset(qmax_c[:], -3.0e38))
    t_e0 = p.do("gpsimd", lambda e: e.memset(e_sb[:], 0.0))
    t_es = []
    for ic in range(8):
        for sub in range(2):
            h = 2 * ic + sub
            t_es.append(p.do("gpsimd", lambda e, ic=ic, sub=sub, h=h: e.memset(
                e_sb[64 * sub:64 * sub + 64, 16 * ic + h:16 * ic + h + 1], 1.0),
                deps=[t_e0]))
    t_th0 = p.do("gpsimd", lambda e: e.iota(
        theta_bc[0:1, :], pattern=[[1, 512]], base=0, channel_multiplier=0,
        allow_small_or_imprecise_dtypes=True))
    t_th1 = p.do("scalar", lambda e: e.activation(
        theta_bc[0:1, :], theta_bc[0:1, :], AF.Exp, bias=0.0, scale=-2.0 * LN1E4),
        deps=[t_th0])
    t_th2 = p.do("vector", lambda e: e.tensor_scalar_mul(
        theta_bc[0:1, :], theta_bc[0:1, :], 1.0 / 512.0), deps=[t_th1])
    t_th3 = p.do("tensor", lambda e: e.matmul(
        PB(0)[:, 0:512], ones8[0:1, :], theta_bc[0:1, :], start=True, stop=True),
        deps=[t_ones, t_th2] + bk.pre_write(0))
    bk.wrote(0, t_th3)
    t_th = p.do("vector", lambda e: e.tensor_copy(theta_bc[:], PB(0)[:, 0:512]),
                deps=[t_th3])
    bk.read(0, t_th)

    t_b = {}
    for nm in ["bdq", "bdkv", "buq", "buk", "b1"]:
        t_b[nm] = p.do("sync", lambda e, nm=nm: e.dma_start(
            out=bcol[nm][:], in_=win[nm].rearrange("(c p) -> p c", p=128),
            allow_slow_non_contiguous=True), dma="bias0")
    t_sg0 = p.do("sync", lambda e: e.dma_start(out=seg_sb[:], in_=seg.unsqueeze(0)),
                 dma="bias0")
    for nm in ["bdq", "bdkv", "buq", "buk", "b1"]:
        t_b[nm] = t_sg0
    t_sg1 = p.do("tensor", lambda e: e.matmul(
        PB(1)[:, 0:2], ones8[0:1, :], seg_sb[:], start=True, stop=True),
        deps=[t_sg0, t_ones] + bk.pre_write(1))
    bk.wrote(1, t_sg1)
    t_seg = p.do("vector", lambda e: e.tensor_copy(segcol[:], PB(1)[:, 0:2]), deps=[t_sg1])
    bk.read(1, t_seg)

    t_buqs = p.do("vector", lambda e: e.tensor_scalar_mul(buqs[:], bcol["buq"][:], INV_SCALE),
                  deps=[t_b["buq"]])

    brow = {}

    def load_brow(stk, nm, ch):
        t = mk_sb(stk)(nm + "_row", [1, D])
        brow[nm] = t
        t_b[nm] = p.do("sync", lambda e, nm=nm, t=t: e.dma_start(
            out=t[:], in_=win[nm].unsqueeze(0)), dma=ch)

    wsb = {}

    def load_w(stk, nm, I, O):
        t = mk_sb(stk)(nm + "_sb", [128, (I // 128) * O])
        toks = []
        for ic in range(I // 128):
            toks.append(p.do("gpsimd", lambda e, ic=ic, t=t, nm=nm, O=O: e.dma_start(
                out=t[:, O * ic:O * (ic + 1)], in_=win[nm][128 * ic:128 * (ic + 1), :]),
                dma="w_" + nm))
        wsb[nm] = (t, O)
        return toks

    WC = lambda nm, ic, lo, hi: wsb[nm][0][:, wsb[nm][1] * ic + lo: wsb[nm][1] * ic + hi]

    def mm_group(bank, mms, extra_deps=()):
        pre = bk.pre_write(bank) + list(extra_deps)
        tok = None
        nmm = len(mms)
        for j, (lf, rf, dd, istr, osl) in enumerate(mms):
            o = PB(bank)[:, 0:512] if osl is None else osl
            tok = p.do("tensor", lambda e, lf=lf, rf=rf, o=o, j=j, istr=istr, nmm=nmm: e.matmul(
                o, lf(), rf(), start=(j == 0), stop=(j == nmm - 1),
                is_transpose=True if istr else None),
                deps=(pre if j == 0 else []) + list(dd))
        bk.wrote(bank, tok)
        return tok

    def ln_stats(src_tok, src_ap, dep_extra=()):
        t1 = p.do("vector", lambda e: e.reduce_sum(ln_c[0][:], src_ap, AX.X),
                  deps=[src_tok] + list(dep_extra))
        t2 = p.do("scalar", lambda e: e.activation(
            sq_scr[:], src_ap, AF.Square, bias=0.0, scale=1.0, accum_out=ln_c[2][:]),
            deps=[src_tok, LU.get("sq_scr")] + list(dep_extra))
        LU["sq_scr"] = t2
        t3 = p.do("vector", lambda e: e.tensor_scalar_mul(ln_c[1][:], ln_c[0][:], 1.0 / 1024.0),
                  deps=[t1])
        t4 = p.do("vector", lambda e: e.tensor_scalar_mul(ln_c[2][:], ln_c[2][:], 1.0 / 1024.0),
                  deps=[t2])
        t5 = p.do("vector", lambda e: e.tensor_tensor(ln_m[:], ln_c[1][:], ln_c[1][:], Op.mult),
                  deps=[t3])
        t6 = p.do("vector", lambda e: e.tensor_tensor(ln_c[2][:], ln_c[2][:], ln_m[:],
                                                      Op.subtract), deps=[t4, t5])
        t7 = p.do("vector", lambda e: e.tensor_scalar_add(ln_c[2][:], ln_c[2][:], EPS), deps=[t6])
        t8 = p.do("scalar", lambda e: e.activation(ln_c[3][:], ln_c[2][:], AF.Sqrt,
                                                   bias=0.0, scale=1.0), deps=[t7])
        t9 = p.do("vector", lambda e: e.reciprocal(ln_c[3][:], ln_c[3][:]), deps=[t8])
        t10 = p.do("vector", lambda e: e.tensor_tensor(ln_m[:], ln_c[1][:], ln_c[3][:], Op.mult),
                   deps=[t9])
        t11 = p.do("vector", lambda e: e.tensor_scalar_mul(ln_m[:], ln_m[:], -1.0), deps=[t10])
        return t11

    astack = contextlib.ExitStack()
    asb = mk_sb(astack)
    xnT_t = asb("xnT_t", [128, 8 * 512])
    ckvT_t = asb("ckvT_t", [128, 4 * 512])
    outT_t = asb("outT_t", [128, 8 * 512])
    big_t = asb("big_t", [128, 4096])
    v_t = asb("v_t", [128, 4 * 1040])
    cos_t = asb("cos_t", [128, 512])
    sin_t = asb("sin_t", [128, 512])
    pos_c = asb("pos_c", [128, 1])
    rr_t = asb("rr_t", [128, 512])
    k1_t = asb("k1_t", [128, 512])
    k2_t = asb("k2_t", [128, 512])
    tmpa = asb("tmpa", [128, 512])
    tmpb = asb("tmpb", [128, 512])

    def cos_sin(r0):
        base = r0 % 2048
        si = r0 // 2048
        ta = p.do("gpsimd", lambda e, base=base: e.iota(
            pos_c[:], pattern=[[0, 1]], base=base, channel_multiplier=1,
            allow_small_or_imprecise_dtypes=True), deps=[LU.get("pos")])
        tb = p.do("vector", lambda e, si=si: e.tensor_tensor(
            pos_c[:], pos_c[:], segcol[:, si:si + 1], Op.add), deps=[ta, t_seg])
        tr = p.do("vector", lambda e: e.tensor_scalar(
            rr_t[:], theta_bc[:], pos_c[:], None, Op.mult), deps=[tb, t_th, LU.get("rr")])
        tk1 = p.do("vector", lambda e: e.tensor_scalar(k1_t[:], rr_t[:], PI, None, Op.is_ge),
                   deps=[tr, LU.get("k1")])
        tk1b = p.do("vector", lambda e: e.tensor_scalar_mul(k1_t[:], k1_t[:], TWO_PI), deps=[tk1])
        tk1c = p.do("vector", lambda e: e.tensor_tensor(tmpa[:], rr_t[:], k1_t[:], Op.subtract),
                    deps=[tk1b, LU.get("tmpa")])
        tsin = p.do("scalar", lambda e: e.activation(sin_t[:], tmpa[:], AF.Sin,
                                                     bias=0.0, scale=1.0),
                    deps=[tk1c, LU.get("sin_w")])
        tq1 = p.do("vector", lambda e: e.tensor_scalar(k2_t[:], rr_t[:], HALF_PI, None,
                                                       Op.is_ge), deps=[tr, LU.get("k2")])
        tq2 = p.do("vector", lambda e: e.tensor_scalar(tmpb[:], rr_t[:], 5.0 * HALF_PI, None,
                                                       Op.is_ge), deps=[tr, LU.get("tmpb")])
        tq3 = p.do("vector", lambda e: e.tensor_tensor(k2_t[:], k2_t[:], tmpb[:], Op.add),
                   deps=[tq1, tq2])
        tq4 = p.do("vector", lambda e: e.tensor_scalar_mul(k2_t[:], k2_t[:], TWO_PI), deps=[tq3])
        tq5 = p.do("vector", lambda e: e.tensor_tensor(tmpb[:], rr_t[:], k2_t[:], Op.subtract),
                   deps=[tq4])
        tcos = p.do("scalar", lambda e: e.activation(cos_t[:], tmpb[:], AF.Sin,
                                                     bias=halfpi_col[:], scale=1.0),
                    deps=[tq5, t_hp, LU.get("cos_w")])
        LU["rr"] = tq5
        LU["k1"] = tk1c
        LU["k2"] = tq5
        LU["tmpa"] = tsin
        LU["tmpb"] = tcos
        LU["pos"] = tr
        return tcos, tsin

    def rope_apply(ps_ap, oh, dst_ap, dep_group, tcos, tsin):
        co = cos_t[:, 256 * oh:256 * oh + 256]
        si = sin_t[:, 256 * oh:256 * oh + 256]
        ev, od = ps_ap[:, 0:512:2], ps_ap[:, 1:512:2]
        dev, dod = dst_ap[:, 0:512:2], dst_ap[:, 1:512:2]
        t1 = p.do("vector", lambda e: e.tensor_tensor(tmpa[:, 0:256], ev, co, Op.mult),
                  deps=[dep_group, tcos, LU.get("tmpa")])
        t2 = p.do("vector", lambda e: e.tensor_tensor(tmpb[:, 0:256], od, si, Op.mult),
                  deps=[dep_group, tsin, LU.get("tmpb")])
        t3 = p.do("vector", lambda e: e.tensor_tensor(dev, tmpa[:, 0:256], tmpb[:, 0:256],
                                                      Op.subtract), deps=[t1, t2])
        t4 = p.do("vector", lambda e: e.tensor_tensor(tmpa[:, 0:256], od, co, Op.mult),
                  deps=[dep_group, t3])
        t5 = p.do("vector", lambda e: e.tensor_tensor(tmpb[:, 0:256], ev, si, Op.mult),
                  deps=[dep_group, t3])
        t6 = p.do("vector", lambda e: e.tensor_tensor(dod, tmpa[:, 0:256], tmpb[:, 0:256],
                                                      Op.add), deps=[t4, t5])
        LU["tmpa"] = t6
        LU["tmpb"] = t6
        return t6, [t1, t2, t4, t5]

    def ln_transpose_tile(r0_tile, spill):
        toks_all = []
        for c in range(4):
            buf = xt[c % 2]
            r0 = r0_tile + 128 * c
            tl = p.do("sync", lambda e, buf=buf, r0=r0: e.dma_start(
                out=buf[:], in_=xv[r0:r0 + 128, :]), deps=[LU.get(("xt", c % 2))],
                dma=f"xt{c % 2}")
            ts = ln_stats(tl, buf[:])
            txn = p.do("scalar", lambda e, buf=buf: e.activation(
                xn[:], buf[:], AF.Identity, bias=ln_m[:], scale=ln_c[3][:]),
                deps=[ts, LU.get("xn")])
            LU[("xt", c % 2)] = txn
            ev_last = None
            for half in range(2):
                bkid = half
                mms = []
                for jj in range(4):
                    j = 4 * half + jj
                    mms.append(((lambda j=j: xn[:, 128 * j:128 * j + 128]),
                                (lambda: ident[:]),
                                [txn, t_id], True, PB(bkid)[:, 128 * jj:128 * jj + 128]))
                g = mm_group(bkid, mms)
                for jj in range(4):
                    j = 4 * half + jj
                    tv = p.do("vector", lambda e, bkid=bkid, jj=jj, j=j, c=c: e.tensor_copy(
                        xnT_t[:, 512 * j + 128 * c: 512 * j + 128 * c + 128],
                        PB(bkid)[:, 128 * jj:128 * jj + 128]),
                        deps=[g, LU.get("xnT_t"), LU.get("xnT_dump"), LU.get("xnT_sp_last")])
                    bk.read(bkid, tv)
                    ev_last = tv
                    toks_all.append(tv)
            LU["xn"] = ev_last
        if spill:
            for j in range(8):
                LU["xnT_sp_last"] = p.do("sync", lambda e, j=j, r0_tile=r0_tile: e.dma_start(
                    out=xnT_hbm[128 * j:128 * j + 128, r0_tile:r0_tile + 512],
                    in_=xnT_t[:, 512 * j:512 * j + 512]), deps=list(toks_all),
                    dma="xnT_sp")
                if d_xnT is not None:
                    LU["xnT_dump"] = p.do("sync", lambda e, j=j, r0_tile=r0_tile: e.dma_start(
                        out=d_xnT[128 * j:128 * j + 128, r0_tile:r0_tile + 512],
                        in_=xnT_t[:, 512 * j:512 * j + 512]), deps=list(toks_all),
                        dma="dpx")
        return toks_all

    wa1 = contextlib.ExitStack()
    load_brow(wa1, "bkr", "biasA1")
    load_brow(wa1, "buv", "biasA1")
    t_b["bkr"] = t_b["buv"]
    tw_dkv = load_w(wa1, "wdkv", D, DKV)
    tw_uk = load_w(wa1, "wuk", DKV, D)
    tw_uv = load_w(wa1, "wuv", DKV, D)
    tw_kr = load_w(wa1, "wkr", D, D)

    vones = []
    for c in range(4):
        for hg in range(16):
            vones.append(p.do("gpsimd", lambda e, c=c, hg=hg: e.memset(
                v_t[:, 1040 * c + 65 * hg + 64: 1040 * c + 65 * hg + 65], 1.0)))
    t_vones = vones[-1]

    tkm = None
    for st in range(8):
        r0_tile = 512 * st
        xnT_toks = ln_transpose_tile(r0_tile, spill=True)
        ckv_evac = []
        for oc in range(4):
            bkid = 2 + (oc % 2)
            mms = []
            for ic in range(8):
                mms.append(((lambda ic=ic, oc=oc: WC("wdkv", ic, 128 * oc, 128 * oc + 128)),
                            (lambda ic=ic: xnT_t[:, 512 * ic:512 * ic + 512]),
                            [tw_dkv[-1], xnT_toks[-1]], False, None))
            g = mm_group(bkid, mms)
            tv = p.do("scalar", lambda e, oc=oc, bkid=bkid: e.activation(
                ckvT_t[:, 512 * oc:512 * oc + 512], PB(bkid)[:, 0:512], AF.Identity,
                bias=bcol["bdkv"][:, oc:oc + 1], scale=1.0),
                deps=[g, t_b["bdkv"], LU.get("ckvT")])
            bk.read(bkid, tv)
            ckv_evac.append(tv)
        LU["ckvT"] = ckv_evac[-1]

        rope_done = []
        for c in range(4):
            tcos, tsin = cos_sin(r0_tile + 128 * c)
            for oh in range(2):
                bkid = 4 + oh
                mms = []
                for ic in range(8):
                    mms.append(((lambda ic=ic, c=c: xnT_t[:, 512 * ic + 128 * c:
                                                          512 * ic + 128 * c + 128]),
                                (lambda ic=ic, oh=oh: WC("wkr", ic, 512 * oh, 512 * oh + 512)),
                                [tw_kr[-1], xnT_toks[-1]], False, None))
                mms.append(((lambda: ones8[0:1, :]),
                            (lambda oh=oh: brow["bkr"][:, 512 * oh:512 * oh + 512]),
                            [t_ones, t_b["bkr"]], False, None))
                g = mm_group(bkid, mms, extra_deps=[LU.get(("krp", c, oh))])
                LU["xnT_t"] = g
                dst = big_t[:, 1024 * c + 512 * oh: 1024 * c + 512 * oh + 512]
                t6, readers = rope_apply(PB(bkid)[:, 0:512], oh, dst, g, tcos, tsin)
                for rt in readers + [t6]:
                    bk.read(bkid, rt)
                LU[("krp", c, oh)] = t6
                rope_done.append(t6)

        kT_evac = []
        for oc in range(8):
            bkid = 6 + (oc % 2)
            mms = []
            for cc in range(4):
                mms.append(((lambda cc=cc, oc=oc: WC("wuk", cc, 128 * oc, 128 * oc + 128)),
                            (lambda cc=cc: ckvT_t[:, 512 * cc:512 * cc + 512]),
                            [tw_uk[-1], ckv_evac[-1]], False, None))
            for c in range(4):
                mms.append(((lambda c=c, oc=oc: big_t[:, 1024 * c + 128 * oc:
                                                      1024 * c + 128 * oc + 128]),
                            (lambda: ident[:]),
                            [rope_done[-1], t_id], True, PB(bkid)[:, 128 * c:128 * c + 128]))
            g = mm_group(bkid, mms)
            tv = p.do("scalar", lambda e, oc=oc, bkid=bkid: e.activation(
                outT_t[:, 512 * oc:512 * oc + 512], PB(bkid)[:, 0:512], AF.Identity,
                bias=bcol["buk"][:, oc:oc + 1], scale=1.0),
                deps=[g, t_b["buk"], LU.get(("outT", oc)), LU.get(("outT_dp", oc))])
            bk.read(bkid, tv)
            tsp = p.do("sync", lambda e, oc=oc, r0_tile=r0_tile: e.dma_start(
                out=kT_hbm[128 * oc:128 * oc + 128, r0_tile:r0_tile + 512],
                in_=outT_t[:, 512 * oc:512 * oc + 512]), deps=[tv], dma=f"osp{oc}")
            LU[("outT", oc)] = tsp
            if d_kT is not None:
                LU[("outT_dp", oc)] = p.do("sync", lambda e, oc=oc, r0_tile=r0_tile: e.dma_start(
                    out=d_kT[128 * oc:128 * oc + 128, r0_tile:r0_tile + 512],
                    in_=outT_t[:, 512 * oc:512 * oc + 512]), deps=[tv], dma=f"dpk{oc}")
            kT_evac.append(tv)

        tsq = p.do("scalar", lambda e: e.activation(big_t[:], outT_t[:], AF.Square,
                                                    bias=0.0, scale=1.0),
                   deps=kT_evac + [rope_done[-1], LU.get("big")])
        mms = []
        for ic in range(8):
            mms.append(((lambda ic=ic: e_sb[:, 16 * ic:16 * ic + 16]),
                        (lambda ic=ic: big_t[:, 512 * ic:512 * ic + 512]),
                        [tsq] + ([t_es[-1]] if ic == 0 else []), False, PB(2)[0:16, 0:512]))
        g = mm_group(2, mms)
        trm = p.do("vector", lambda e: e.reduce_max(t16[:], PB(2)[0:16, 0:512], AX.X),
                   deps=[g, LU.get("t16")])
        bk.read(2, trm)
        tkm = p.do("vector", lambda e: e.tensor_max(kmax_c[:], kmax_c[:], t16[:]),
                   deps=[trm, t_kin])
        LU["t16"] = tkm
        LU["big"] = trm

        for c in range(4):
            ev_t = []
            for oh in range(2):
                bkid = 4 + oh
                mms = []
                for cc in range(4):
                    mms.append(((lambda cc=cc, c=c: ckvT_t[:, 512 * cc + 128 * c:
                                                           512 * cc + 128 * c + 128]),
                                (lambda cc=cc, oh=oh: WC("wuv", cc, 512 * oh, 512 * oh + 512)),
                                [tw_uv[-1], ckv_evac[-1]], False, None))
                mms.append(((lambda: ones8[0:1, :]),
                            (lambda oh=oh: brow["buv"][:, 512 * oh:512 * oh + 512]),
                            [t_ones, t_b["buv"]], False, None))
                g = mm_group(bkid, mms, extra_deps=[LU.get(("vev", c, oh))])
                for hh in range(8):
                    hg = 8 * oh + hh
                    tv = p.do("vector", lambda e, c=c, hh=hh, hg=hg, bkid=bkid: e.tensor_copy(
                        v_t[:, 1040 * c + 65 * hg: 1040 * c + 65 * hg + 64],
                        PB(bkid)[:, 64 * hh:64 * hh + 64]),
                        deps=[g, t_vones, LU.get(("vsp", c)), LU.get(("vdp", c))])
                    bk.read(bkid, tv)
                    ev_t.append(tv)
                LU[("vev", c, oh)] = ev_t[-1]
            tsp = p.do("sync", lambda e, c=c, r0_tile=r0_tile: e.dma_start(
                out=v_hbm[r0_tile + 128 * c: r0_tile + 128 * c + 128, :],
                in_=v_t[:, 1040 * c:1040 * c + 1040]), deps=ev_t, dma=f"vsp{c}")
            LU[("vsp", c)] = tsp
            if d_v is not None:
                LU[("vdp", c)] = p.do("sync", lambda e, c=c, r0_tile=r0_tile: e.dma_start(
                    out=d_v[r0_tile + 128 * c: r0_tile + 128 * c + 128, :],
                    in_=v_t[:, 1040 * c:1040 * c + 1040]), deps=ev_t, dma=f"dpv{c}")

    p.barrier()
    wa1.close()

    wa2 = contextlib.ExitStack()
    load_brow(wa2, "bqr", "biasA2")
    tw_dq = load_w(wa2, "wdq", D, DQ)
    tw_uq = load_w(wa2, "wuq", DQ, D)
    tw_qr = load_w(wa2, "wqr", DQ, D)

    tqm = None
    for stq in range(4):
        r0 = 512 * stq
        xnT_toks = []
        for j in range(8):
            xnT_toks.append(p.do("sync", lambda e, j=j, r0=r0: e.dma_start(
                out=xnT_t[:, 512 * j:512 * j + 512],
                in_=xnT_hbm[128 * j:128 * j + 128, r0:r0 + 512]),
                deps=[LU.get("xnT_t")], dma="xnT_rl"))

        cq_evac = []
        for oc in range(4):
            bkid = 2 + (oc % 2)
            mms = []
            for ic in range(8):
                mms.append(((lambda ic=ic, oc=oc: WC("wdq", ic, 128 * oc, 128 * oc + 128)),
                            (lambda ic=ic: xnT_t[:, 512 * ic:512 * ic + 512]),
                            [tw_dq[-1], xnT_toks[-1]], False, None))
            g = mm_group(bkid, mms)
            tv = p.do("scalar", lambda e, oc=oc, bkid=bkid: e.activation(
                ckvT_t[:, 512 * oc:512 * oc + 512], PB(bkid)[:, 0:512], AF.Identity,
                bias=bcol["bdq"][:, oc:oc + 1], scale=1.0),
                deps=[g, t_b["bdq"], LU.get("ckvT")])
            bk.read(bkid, tv)
            cq_evac.append(tv)
            LU["xnT_t"] = g
        LU["ckvT"] = cq_evac[-1]

        rope_done = []
        for c in range(4):
            tcos, tsin = cos_sin(r0 + 128 * c)
            for oh in range(2):
                bkid = 4 + oh
                mms = []
                for cc in range(4):
                    mms.append(((lambda cc=cc, c=c: ckvT_t[:, 512 * cc + 128 * c:
                                                           512 * cc + 128 * c + 128]),
                                (lambda cc=cc, oh=oh: WC("wqr", cc, 512 * oh, 512 * oh + 512)),
                                [tw_qr[-1], cq_evac[-1]], False, None))
                mms.append(((lambda: ones8[0:1, :]),
                            (lambda oh=oh: brow["bqr"][:, 512 * oh:512 * oh + 512]),
                            [t_ones, t_b["bqr"]], False, None))
                g = mm_group(bkid, mms, extra_deps=[LU.get(("krp", c, oh))])
                dst = big_t[:, 1024 * c + 512 * oh: 1024 * c + 512 * oh + 512]
                t6, readers = rope_apply(PB(bkid)[:, 0:512], oh, dst, g, tcos, tsin)
                for rt in readers + [t6]:
                    bk.read(bkid, rt)
                LU[("krp", c, oh)] = t6
                rope_done.append(t6)

        qT_evac = []
        for oc in range(8):
            bkid = 6 + (oc % 2)
            mms = []
            for cc in range(4):
                mms.append(((lambda cc=cc, oc=oc: WC("wuq", cc, 128 * oc, 128 * oc + 128)),
                            (lambda cc=cc: ckvT_t[:, 512 * cc:512 * cc + 512]),
                            [tw_uq[-1], cq_evac[-1]], False, None))
            for c in range(4):
                mms.append(((lambda c=c, oc=oc: big_t[:, 1024 * c + 128 * oc:
                                                      1024 * c + 128 * oc + 128]),
                            (lambda: ident[:]),
                            [rope_done[-1], t_id], True, PB(bkid)[:, 128 * c:128 * c + 128]))
            g = mm_group(bkid, mms)
            tv = p.do("scalar", lambda e, oc=oc, bkid=bkid: e.activation(
                outT_t[:, 512 * oc:512 * oc + 512], PB(bkid)[:, 0:512], AF.Identity,
                bias=buqs[:, oc:oc + 1], scale=INV_SCALE),
                deps=[g, t_buqs, LU.get(("outT", oc)), LU.get(("outT_dp", oc))])
            bk.read(bkid, tv)
            tsp = p.do("sync", lambda e, oc=oc, r0=r0: e.dma_start(
                out=qT_hbm[128 * oc:128 * oc + 128, r0:r0 + 512],
                in_=outT_t[:, 512 * oc:512 * oc + 512]), deps=[tv], dma=f"osp{oc}")
            LU[("outT", oc)] = tsp
            if d_qT is not None:
                LU[("outT_dp", oc)] = p.do("sync", lambda e, oc=oc, r0=r0: e.dma_start(
                    out=d_qT[128 * oc:128 * oc + 128, r0:r0 + 512],
                    in_=outT_t[:, 512 * oc:512 * oc + 512]), deps=[tv], dma=f"dpk{oc}")
            qT_evac.append(tv)

        tsq = p.do("scalar", lambda e: e.activation(big_t[:], outT_t[:], AF.Square,
                                                    bias=0.0, scale=1.0),
                   deps=qT_evac + [rope_done[-1], LU.get("big")])
        mms = []
        for ic in range(8):
            mms.append(((lambda ic=ic: e_sb[:, 16 * ic:16 * ic + 16]),
                        (lambda ic=ic: big_t[:, 512 * ic:512 * ic + 512]),
                        [tsq], False, PB(2)[0:16, 0:512]))
        g = mm_group(2, mms)
        trm = p.do("vector", lambda e: e.reduce_max(t16[:], PB(2)[0:16, 0:512], AX.X),
                   deps=[g, LU.get("t16")])
        bk.read(2, trm)
        tqm = p.do("vector", lambda e: e.tensor_max(qmax_c[:], qmax_c[:], t16[:]),
                   deps=[trm, t_qin])
        LU["t16"] = tqm
        LU["big"] = trm

    g1 = mm_group(3, [((lambda: kmax_c[:]), (lambda: ident[0:16, 0:16]),
                       [tkm, tqm, t_id], True, PB(3)[0:1, 0:16])])
    tkr = p.do("vector", lambda e: e.tensor_copy(mrow[:, 0:16], PB(3)[0:1, 0:16]), deps=[g1])
    bk.read(3, tkr)
    g2 = mm_group(3, [((lambda: qmax_c[:]), (lambda: ident[0:16, 0:16]),
                       [tqm, t_id], True, PB(3)[0:1, 0:16])])
    tqr2 = p.do("vector", lambda e: e.tensor_copy(mrow[:, 16:32], PB(3)[0:1, 0:16]), deps=[g2])
    bk.read(3, tqr2)
    tpr = p.do("vector", lambda e: e.tensor_tensor(
        mrow[:, 0:16], mrow[:, 0:16], mrow[:, 16:32], Op.mult), deps=[tkr, tqr2])
    tm2 = p.do("vector", lambda e: e.reduce_max(mrow[:, 32:33], mrow[:, 0:16], AX.X), deps=[tpr])
    tm1 = p.do("scalar", lambda e: e.activation(mrow[:, 32:33], mrow[:, 32:33], AF.Sqrt,
                                                bias=0.0, scale=1.0), deps=[tm2])
    tmn = p.do("vector", lambda e: e.tensor_scalar_mul(mrow[:, 32:33], mrow[:, 32:33], -1.0),
               deps=[tm1])
    g3 = mm_group(3, [((lambda: ones8[0:1, :]), (lambda: mrow[:, 32:33]),
                       [tmn, t_ones], False, PB(3)[:, 0:1])])
    t_negm = p.do("vector", lambda e: e.tensor_copy(negM[:], PB(3)[:, 0:1]), deps=[g3])
    bk.read(3, t_negm)
    if d_negm is not None:
        p.do("sync", lambda e: e.dma_start(out=d_negm[:, :], in_=negM[0:1, :]),
             deps=[t_negm], dma="dump")

    p.barrier()
    wa2.close()
    astack.close()

    cdstack = contextlib.ExitStack()
    xn2T_sb = mk_sb(cdstack)("xn2T_sb", [128, 8 * 2048])
    bcstack = contextlib.ExitStack()
    OT_sb = mk_sb(bcstack)("OT_sb", [128, 8 * 2048])
    bstack = contextlib.ExitStack()
    bsb = mk_sb(bstack)
    qt_sb = bsb("qt_sb", [128, 2 * 2048])
    kt_sb = [bsb(f"kt_sb{i}", [128, 256]) for i in range(2)]
    vt_sb = [bsb(f"vt_sb{i}", [128, 260]) for i in range(2)]
    pt_sb = bsb("pt_sb", [128, 4 * 512])
    o32 = [bsb(f"o32_{i}", [64, 512]) for i in range(2)]
    lrec_row = bsb("lrec_row", [1, 512])

    tev = None
    for Q in range(4):
        qload = []
        for hp in range(2):
            qload.append(p.do("sync", lambda e, Q=Q, hp=hp: e.dma_start(
                out=qt_sb[:, 2048 * hp:2048 * hp + 2048],
                in_=qT_hbm[256 * Q + 128 * hp:256 * Q + 128 * hp + 128, :]),
                deps=[LU.get("qt_sb")], dma=f"qld{hp}"))
        for qt in range(4):
            av_tok = [None] * 4
            sc_rot = 0
            for kc in range(32):
                bufi = kc % 2
                tk = p.do("sync", lambda e, Q=Q, kc=kc, bufi=bufi: e.dma_start(
                    out=kt_sb[bufi][:].rearrange("p (a k) -> p a k", a=2),
                    in_=kT_hbm[256 * Q:256 * Q + 256, 128 * kc:128 * kc + 128]
                        .rearrange("(a p) k -> p a k", p=128)),
                    deps=[LU.get(("kt", bufi))], dma=f"kt{bufi}")
                tvl = p.do("sync", lambda e, Q=Q, kc=kc, bufi=bufi: e.dma_start(
                    out=vt_sb[bufi][:],
                    in_=v_hbm[128 * kc:128 * kc + 128, 260 * Q:260 * Q + 260]),
                    deps=[LU.get(("vt", bufi))], dma=f"vt{bufi}")
                kt_last = vt_last = None
                for hp in range(2):
                    sb0 = 4 + (sc_rot % 3)
                    sb1 = 4 + ((sc_rot + 1) % 3)
                    sc_rot += 2
                    tA = p.do("tensor", lambda e, hp=hp, qt=qt, bufi=bufi, sb0=sb0: e.matmul(
                        PB(sb0)[:, 0:512],
                        kt_sb[bufi][0:64, 128 * hp:128 * hp + 128],
                        qt_sb[0:64, 2048 * hp + 512 * qt: 2048 * hp + 512 * qt + 512],
                        start=True, stop=True),
                        deps=[tk, qload[0], qload[1]] + bk.pre_write(sb0))
                    bk.wrote(sb0, tA)
                    tB = p.do("tensor", lambda e, hp=hp, qt=qt, bufi=bufi, sb1=sb1: e.matmul(
                        PB(sb1)[:, 0:512],
                        kt_sb[bufi][64:128, 128 * hp:128 * hp + 128],
                        qt_sb[64:128, 2048 * hp + 512 * qt: 2048 * hp + 512 * qt + 512],
                        start=True, stop=True, tile_position=(64, 0)),
                        deps=[tk, qload[0], qload[1]] + bk.pre_write(sb1))
                    bk.wrote(sb1, tB)
                    kt_last = tB
                    for ab in range(2):
                        hh = 2 * hp + ab
                        sbb = sb0 if ab == 0 else sb1
                        gtok = tA if ab == 0 else tB
                        slot = hh
                        tex = p.do("scalar", lambda e, sbb=sbb, slot=slot: e.activation(
                            pt_sb[:, 512 * slot:512 * slot + 512], PB(sbb)[:, 0:512],
                            AF.Exp, bias=negM[:], scale=1.0),
                            deps=[gtok, t_negm, LU.get(("pt", slot))])
                        bk.read(sbb, tex)
                        dd = [tex, tvl]
                        if kc == 0:
                            dd += bk.pre_write(hh)
                        tav = p.do("tensor", lambda e, hh=hh, slot=slot, bufi=bufi, kc=kc:
                                   e.matmul(
                            PB(hh)[0:65, 0:512],
                            vt_sb[bufi][:, 65 * hh:65 * hh + 65],
                            pt_sb[:, 512 * slot:512 * slot + 512],
                            start=(kc == 0), stop=(kc == 31)), deps=dd)
                        if kc == 0:
                            bk.wrote(hh, tav)
                        av_tok[hh] = tav
                        LU[("pt", slot)] = tav
                        vt_last = tav
                LU[("kt", bufi)] = kt_last
                LU[("vt", bufi)] = vt_last
            for hh in range(4):
                g = 4 * Q + hh
                fc = g // 2
                prow = 64 * (g % 2)
                trc = p.do("vector", lambda e, hh=hh: e.reciprocal(
                    lrec_row[:], PB(hh)[64:65, 0:512]),
                    deps=[av_tok[hh], LU.get("lrec_row")])
                bk.read(hh, trc)
                gbc = mm_group(7, [((lambda: ones8[0:1, 0:64]), (lambda: lrec_row[:]),
                                    [trc, t_ones], False, PB(7)[0:64, 0:512])])
                obuf = o32[hh % 2]
                tcp = p.do("vector", lambda e, hh=hh, obuf=obuf: e.tensor_copy(
                    obuf[:], PB(hh)[0:64, 0:512]),
                    deps=[av_tok[hh], LU.get(("o32", hh % 2))])
                bk.read(hh, tcp)
                tnm = p.do("vector", lambda e, obuf=obuf: e.tensor_tensor(
                    obuf[:], obuf[:], PB(7)[0:64, 0:512], Op.mult), deps=[tcp, gbc])
                bk.read(7, tnm)
                LU["lrec_row"] = tnm
                tev = p.do("gpsimd", lambda e, fc=fc, prow=prow, qt=qt, obuf=obuf: e.dma_start(
                    out=OT_sb[prow:prow + 64, 2048 * fc + 512 * qt: 2048 * fc + 512 * qt + 512],
                    in_=obuf[:]), deps=[tnm], dma=f"oev{hh % 2}")
                LU[("o32", hh % 2)] = tev
        LU["qt_sb"] = tev

    if d_OT is not None:
        for j in range(8):
            p.do("sync", lambda e, j=j: e.dma_start(
                out=d_OT[128 * j:128 * j + 128, :], in_=OT_sb[:, 2048 * j:2048 * j + 2048]),
                deps=[tev], dma="dump")

    p.barrier()
    bstack.close()

    wcstack = contextlib.ExitStack()
    load_brow(wcstack, "bo", "biasC")
    tw_o = load_w(wcstack, "wo", D, D)

    for qc in range(16):
        ao_toks = []
        for oh in range(2):
            bkid = oh
            mms = []
            for dc in range(8):
                mms.append(((lambda dc=dc, qc=qc: OT_sb[:, 2048 * dc + 128 * qc:
                                                        2048 * dc + 128 * qc + 128]),
                            (lambda dc=dc, oh=oh: WC("wo", dc, 512 * oh, 512 * oh + 512)),
                            [tw_o[-1]], False, None))
            mms.append(((lambda: ones8[0:1, :]),
                        (lambda oh=oh: brow["bo"][:, 512 * oh:512 * oh + 512]),
                        [t_ones, t_b["bo"]], False, None))
            g = mm_group(bkid, mms)
            tv = p.do("scalar", lambda e, oh=oh, bkid=bkid: e.activation(
                ao_t[:, 512 * oh:512 * oh + 512], PB(bkid)[:, 0:512], AF.Identity,
                bias=0.0, scale=1.0), deps=[g, LU.get("ao")])
            bk.read(bkid, tv)
            ao_toks.append(tv)
        LU["ao"] = ao_toks[-1]
        ts = ln_stats(ao_toks[-1], ao_t[:], dep_extra=[ao_toks[0]])
        tln = p.do("scalar", lambda e: e.activation(
            lnout[:], ao_t[:], AF.Identity, bias=ln_m[:], scale=ln_c[3][:]),
            deps=[ts, ao_toks[0], ao_toks[1], LU.get("lnout")])
        LU["ao"] = tln
        tl = p.do("sync", lambda e, qc=qc: e.dma_start(
            out=xt[0][:], in_=xv[128 * qc:128 * qc + 128, :]),
            deps=[LU.get(("xt", 0))], dma="xt0")
        tres = p.do("vector", lambda e: e.tensor_tensor(res_t[:], xt[0][:], lnout[:], Op.add),
                    deps=[tl, tln, LU.get("res_t"), LU.get("res_dump"), LU.get("res_sp_last")])
        LU[("xt", 0)] = tres
        LU["lnout"] = tres
        LU["res_sp_last"] = p.do("sync", lambda e, qc=qc: e.dma_start(
            out=res_hbm[128 * qc:128 * qc + 128, :], in_=res_t[:]), deps=[tres], dma="res_sp")
        if d_res is not None:
            LU["res_dump"] = p.do("sync", lambda e, qc=qc: e.dma_start(
                out=d_res[128 * qc:128 * qc + 128, :], in_=res_t[:]), deps=[tres], dma="dpr")
        ts2 = ln_stats(tres, res_t[:])
        txn2 = p.do("scalar", lambda e: e.activation(
            xn[:], res_t[:], AF.Identity, bias=ln_m[:], scale=ln_c[3][:]),
            deps=[ts2, LU.get("xn")])
        LU["res_t"] = txn2
        ev_last = None
        for half in range(2):
            bkid = 2 + half
            mms = []
            for jj in range(4):
                j = 4 * half + jj
                mms.append(((lambda j=j: xn[:, 128 * j:128 * j + 128]),
                            (lambda: ident[:]),
                            [txn2, t_id], True, PB(bkid)[:, 128 * jj:128 * jj + 128]))
            g = mm_group(bkid, mms)
            for jj in range(4):
                j = 4 * half + jj
                tv = p.do("vector", lambda e, j=j, jj=jj, qc=qc, bkid=bkid: e.tensor_copy(
                    xn2T_sb[:, 2048 * j + 128 * qc: 2048 * j + 128 * qc + 128],
                    PB(bkid)[:, 128 * jj:128 * jj + 128]), deps=[g])
                bk.read(bkid, tv)
                ev_last = tv
        LU["xn"] = ev_last

    p.barrier()
    wcstack.close()
    bcstack.close()

    dstack = contextlib.ExitStack()
    dsb = mk_sb(dstack)
    wr_raw = dsb("wr_raw", [8, 1024])
    wrT_sb = dsb("wrT_sb", [128, 8 * 8])
    ez_sb = dsb("ez_sb", [128, 8])
    zcol = dsb("zcol", [128, 1])
    y_acc = dsb("y_acc", [128, 16 * 1024])
    w1_sb = [dsb(f"w1_sb{i}", [128, 8 * 256]) for i in range(2)]
    w2_sb = [dsb(f"w2_sb{i}", [128, 2 * 1024]) for i in range(2)]
    h_sb = dsb("h_sb", [128, 2 * 512])

    load_brow(dstack, "b2", "biasD")
    twr = p.do("gpsimd", lambda e: e.dma_start(out=wr_raw[:], in_=win["wr"][:, :]), dma="wr")
    t_wrT = None
    for ic in range(8):
        g = mm_group(4, [((lambda ic=ic: wr_raw[0:8, 128 * ic:128 * ic + 128]),
                          (lambda: ident[0:8, 0:8]),
                          [twr, t_id], True, PB(4)[:, 0:8])])
        t_wrT = p.do("vector", lambda e, ic=ic: e.tensor_copy(
            wrT_sb[:, 8 * ic:8 * ic + 8], PB(4)[:, 0:8]), deps=[g])
        bk.read(4, t_wrT)

    for qc in range(16):
        mms = []
        for ic in range(8):
            mms.append(((lambda ic=ic, qc=qc: xn2T_sb[:, 2048 * ic + 128 * qc:
                                                      2048 * ic + 128 * qc + 128]),
                        (lambda ic=ic: wrT_sb[:, 8 * ic:8 * ic + 8]),
                        [t_wrT], False, PB(4)[:, 0:8]))
        g = mm_group(4, mms)
        tmx = p.do("vector", lambda e: e.reduce_max(zcol[:], PB(4)[:, 0:8], AX.X, negate=True),
                   deps=[g, LU.get("zcol")])
        bk.read(4, tmx)
        tex = p.do("scalar", lambda e: e.activation(
            ez_sb[:], PB(4)[:, 0:8], AF.Exp, bias=zcol[:], scale=1.0, accum_out=zcol[:]),
            deps=[tmx, LU.get("ez")])
        bk.read(4, tex)
        trc = p.do("vector", lambda e: e.reciprocal(zcol[:], zcol[:]), deps=[tex])
        tp7 = p.do("vector", lambda e, qc=qc: e.tensor_tensor(
            p7_sb[:, qc:qc + 1], ez_sb[:, 7:8], zcol[:], Op.mult), deps=[trc])
        LU["zcol"] = tp7
        LU["ez"] = tp7

    tz = p.do("vector", lambda e: e.memset(y_acc[:], 0.0))
    yacc_tok = {qc: tz for qc in range(16)}

    for f in range(16):
        bufi = f % 2
        tw1 = []
        for ic in range(8):
            tw1.append(p.do("gpsimd", lambda e, f=f, ic=ic, bufi=bufi: e.dma_start(
                out=w1_sb[bufi][:, 256 * ic:256 * ic + 256],
                in_=win["w1"][128 * ic:128 * ic + 128, 256 * f:256 * f + 256]),
                deps=[LU.get(("w1", bufi))], dma=f"w1_{bufi}"))
        tw2 = []
        for fs in range(2):
            tw2.append(p.do("gpsimd", lambda e, f=f, fs=fs, bufi=bufi: e.dma_start(
                out=w2_sb[bufi][:, 1024 * fs:1024 * fs + 1024],
                in_=win["w2"][256 * f + 128 * fs:256 * f + 128 * fs + 128, :]),
                deps=[LU.get(("w2", bufi))], dma=f"w2_{bufi}"))
        ylast = None
        for qt in range(4):
            h_toks = []
            for fs in range(2):
                bkid = 4 + fs
                mms = []
                for ic in range(8):
                    mms.append(((lambda ic=ic, fs=fs, bufi=bufi: w1_sb[bufi][
                                    :, 256 * ic + 128 * fs: 256 * ic + 128 * fs + 128]),
                                (lambda ic=ic, qt=qt: xn2T_sb[:, 2048 * ic + 512 * qt:
                                                              2048 * ic + 512 * qt + 512]),
                                [tw1[-1]], False, None))
                g = mm_group(bkid, mms)
                th = p.do("scalar", lambda e, fs=fs, f=f, bkid=bkid: e.activation(
                    h_sb[:, 512 * fs:512 * fs + 512], PB(bkid)[:, 0:512], AF.Gelu,
                    bias=bcol["b1"][:, 2 * f + fs:2 * f + fs + 1], scale=1.0),
                    deps=[g, t_b["b1"], LU.get(("h", fs))])
                bk.read(bkid, th)
                h_toks.append(th)
            for qs in range(4):
                qc = 4 * qt + qs
                for oh in range(2):
                    bkid = 2 * (qs % 2) + oh
                    mms = []
                    for fs in range(2):
                        mms.append(((lambda fs=fs, qs=qs: h_sb[:, 512 * fs + 128 * qs:
                                                               512 * fs + 128 * qs + 128]),
                                    (lambda fs=fs, oh=oh, bufi=bufi: w2_sb[bufi][
                                        :, 1024 * fs + 512 * oh: 1024 * fs + 512 * oh + 512]),
                                    [h_toks[fs], tw2[-1]], False, None))
                    if f == 15:
                        mms.append(((lambda: ones8[0:1, :]),
                                    (lambda oh=oh: brow["b2"][:, 512 * oh:512 * oh + 512]),
                                    [t_ones, t_b["b2"]], False, None))
                    g = mm_group(bkid, mms)
                    ty = p.do("vector", lambda e, qc=qc, oh=oh, bkid=bkid: e.tensor_tensor(
                        y_acc[:, 1024 * qc + 512 * oh: 1024 * qc + 512 * oh + 512],
                        y_acc[:, 1024 * qc + 512 * oh: 1024 * qc + 512 * oh + 512],
                        PB(bkid)[:, 0:512], Op.add),
                        deps=[g, yacc_tok[qc]])
                    bk.read(bkid, ty)
                    yacc_tok[qc] = ty
                    ylast = ty
            LU[("h", 0)] = ylast
            LU[("h", 1)] = ylast
        LU[("w1", bufi)] = ylast
        LU[("w2", bufi)] = ylast

    for qc in range(16):
        tyb = p.do("vector", lambda e, qc=qc: e.tensor_scalar(
            ao_t[:], y_acc[:, 1024 * qc:1024 * qc + 1024], p7_sb[:, qc:qc + 1], None, Op.mult),
            deps=[yacc_tok[qc], LU.get("ao")])
        LU["ao"] = tyb
        ts = ln_stats(tyb, ao_t[:])
        tln = p.do("scalar", lambda e: e.activation(
            lnout[:], ao_t[:], AF.Identity, bias=ln_m[:], scale=ln_c[3][:]),
            deps=[ts, tyb, LU.get("lnout")])
        LU["ao"] = tln
        trl = p.do("sync", lambda e, qc=qc: e.dma_start(
            out=res_t[:], in_=res_hbm[128 * qc:128 * qc + 128, :]),
            deps=[LU.get("res_t"), LU.get("out_last")], dma="res_rl")
        tout = p.do("vector", lambda e: e.tensor_tensor(res_t[:], res_t[:], lnout[:], Op.add),
                    deps=[trl, tln])
        LU["res_t"] = tout
        LU["lnout"] = tout
        LU["out_last"] = p.do("sync", lambda e, qc=qc: e.dma_start(
            out=out_hbm[128 * qc:128 * qc + 128, :], in_=res_t[:]), deps=[tout], dma="out")

    dstack.close()
    cdstack.close()

    with gstack:
        with nc.Block() as block:
            p.emit(nc, gstack, block, csem)
    return nc


_cache = {}


def kernel(**kw):
    x = np.asarray(kw["L0_attn_x"], np.float32)
    if "nc" not in _cache:
        _cache["nc"] = build_program()
    nc = _cache["nc"]
    shared = {
        "wdq": kw["L0_attn_Wd_q_w"], "wuq": kw["L0_attn_Wu_q_w"],
        "wqr": kw["L0_attn_Wq_r_w"], "wkr": kw["L0_attn_Wk_r_w"],
        "wdkv": kw["L0_attn_Wd_kv_w"], "wuk": kw["L0_attn_Wu_k_w"],
        "wuv": kw["L0_attn_Wu_v_w"], "wo": kw["L0_attn_Wo_w"],
        "wr": kw["L0_ffn_Wr"], "w1": kw["L0_ffn_W1"][7], "w2": kw["L0_ffn_W2"][7],
        "bdq": kw["L0_attn_Wd_q_b"], "buq": kw["L0_attn_Wu_q_b"],
        "bqr": kw["L0_attn_Wq_r_b"], "bkr": kw["L0_attn_Wk_r_b"],
        "bdkv": kw["L0_attn_Wd_kv_b"], "buk": kw["L0_attn_Wu_k_b"],
        "buv": kw["L0_attn_Wu_v_b"], "bo": kw["L0_attn_Wo_b"],
        "b1": kw["L0_ffn_b1"][7], "b2": kw["L0_ffn_b2"][7],
    }
    shared = {k: np.ascontiguousarray(np.asarray(v, np.float32)) for k, v in shared.items()}
    in_maps = []
    for core in range(8):
        b, hf = core // 2, core % 2
        q0 = NQ * hf
        xvv = np.ascontiguousarray(np.concatenate([x[b, q0:], x[b, :q0]], axis=0))
        m = dict(shared)
        m["xv"] = xvv
        m["seg"] = np.array([q0, (q0 + NQ) % S], np.float32)
        in_maps.append(m)
    res = run_bass_kernel_spmd(nc, in_maps, core_ids=list(range(8)))
    r0 = np.asarray(res.results[0]["out"])
    out = np.empty((B, S, D), r0.dtype)
    for core in range(8):
        b, hf = core // 2, core % 2
        out[b, NQ * hf:NQ * hf + NQ] = np.asarray(res.results[core]["out"])
    return out
